# version 23
# baseline (speedup 1.0000x reference)
import contextlib
import numpy as np
import concourse.bass as bass
import concourse.mybir as mybir
from concourse.bass_utils import run_bass_kernel_spmd

F32 = mybir.dt.float32
BF16 = mybir.dt.bfloat16
F8 = mybir.dt.float8e4
AF = mybir.ActivationFunctionType
ALU = mybir.AluOpType

P = 128
T = 2048
D = 2048
F = 8192
NT = T // P
ND = D // P
NF = F // P
TAU = 512
NTAU = T // TAU
EPS = 1e-5
NW = float(F * D)
NOPS = 0
AXX = mybir.AxisListType.X
STAGE = 99
QKV_BIAS = True


class Sem:
    def __init__(self, es, nc, name):
        self.h = es.enter_context(nc.semaphore(name))
        self.n = 0


def build_nc():
    nc = bass.Bass("TRN2", target_bir_lowering=False)

    ap = lambda t: t.ap()
    x_d = ap(nc.dram_tensor("x", [T, D], F32, kind="ExternalInput"))
    win_d = ap(nc.dram_tensor("w_in", [P, 24, ND, 256], F32, kind="ExternalInput"))
    wout_d = ap(nc.dram_tensor("w_out", [P, 8, ND, 256], F32, kind="ExternalInput"))
    w1t_d = ap(nc.dram_tensor("w1t", [D, F], F32, kind="ExternalInput"))
    w2t_d = ap(nc.dram_tensor("w2t", [F, D], F32, kind="ExternalInput"))
    w1sh_d = ap(nc.dram_tensor("w1sh", [D, F // 8], F32, kind="ExternalInput"))
    w2sh_d = ap(nc.dram_tensor("w2sh", [F // 8, D], F32, kind="ExternalInput"))
    gam_d = ap(nc.dram_tensor("gam", [P, D], F32, kind="ExternalInput"))
    bet_d = ap(nc.dram_tensor("bet", [P, D], F32, kind="ExternalInput"))
    b1pp_d = ap(nc.dram_tensor("b1pp", [P, 64], F32, kind="ExternalInput"))
    b2pp_d = ap(nc.dram_tensor("b2pp", [P, 16], F32, kind="ExternalInput"))
    bout_d = ap(nc.dram_tensor("bout", [1, D], F32, kind="ExternalInput"))
    binpp_d = ap(nc.dram_tensor("binpp", [P, 48], F32, kind="ExternalInput"))
    cos_d = ap(nc.dram_tensor("cosT", [T, 512], F32, kind="ExternalInput"))
    sin_d = ap(nc.dram_tensor("sinT", [T, 512], F32, kind="ExternalInput"))
    iden_d = ap(nc.dram_tensor("iden", [P, P], F32, kind="ExternalInput"))
    out_d = ap(nc.dram_tensor("out", [T, D], F32, kind="ExternalOutput"))
    win16s_d = ap(nc.dram_tensor("win16s", [24, P, ND, 256], BF16, kind="Internal"))
    wout16s_d = ap(nc.dram_tensor("wout16s", [4, P, ND, 512], BF16, kind="Internal"))
    w1q_d = ap(nc.dram_tensor("w1q", [16, P, ND, 512], BF16, kind="Internal"))
    w2q_d = ap(nc.dram_tensor("w2q", [16, P, NF, P], BF16, kind="Internal"))
    qts_d = ap(nc.dram_tensor("qtd", [4, P, ND, 512], BF16, kind="Internal"))
    ccs_i = ap(nc.dram_tensor("ccs_i", [1, 2], F32, kind="Internal"))
    ccs_o = ap(nc.dram_tensor("ccs_o", [1, 2], F32, kind="Internal"))
    cca_i = ap(nc.dram_tensor("cca_i", [P, 16], F32, kind="Internal"))
    cca_o = ap(nc.dram_tensor("cca_o", [P, 16], F32, kind="Internal"))
    absd = ap(nc.dram_tensor("absd", [1, 256], F32, kind="Internal"))

    with contextlib.ExitStack() as es:
        E = es.enter_context
        m_all = E(nc.sbuf_tensor("m_all", [P, NT], F32))
        r1_all = E(nc.sbuf_tensor("r1_all", [P, NT], F32))
        r2_all = E(nc.sbuf_tensor("r2_all", [P, NT], F32))
        A_acc = E(nc.sbuf_tensor("A_acc", [P, 16], F32))
        A_one = E(nc.sbuf_tensor("A_one", [P, 16], F32))
        ident16 = E(nc.sbuf_tensor("ident16", [P, P], BF16))
        ones_c = E(nc.sbuf_tensor("ones_c", [P, 1], F32))
        ones_r = E(nc.sbuf_tensor("ones_r", [1, P], F32))
        ones_r16 = E(nc.sbuf_tensor("ones_r16", [1, P], BF16))
        ones2_16 = E(nc.sbuf_tensor("ones2_16", [P, 2], BF16))
        b1pp = E(nc.sbuf_tensor("sb_b1pp", [P, 64], F32))
        b2pp = E(nc.sbuf_tensor("sb_b2pp", [P, 16], F32))
        bout16 = E(nc.sbuf_tensor("bout16", [1, D], BF16))
        scrow = E(nc.sbuf_tensor("scrow", [1, 2], F32))
        thpos = E(nc.sbuf_tensor("thpos", [P, 2], F32))
        thneg = E(nc.sbuf_tensor("thneg", [P, 2], F32))
        stt1 = E(nc.sbuf_tensor("stt1", [P, 1], F32))
        stt2 = E(nc.sbuf_tensor("stt2", [P, 1], F32))
        psA = E(nc.psum_tensor("psA", [P, 3072], F32))
        psT = E(nc.psum_tensor("psT", [P, 2048], BF16))

        sLd = Sem(es, nc, "sLd")
        sSc = Sem(es, nc, "sSc")
        sGp = Sem(es, nc, "sGp")
        sPe = Sem(es, nc, "sPe")
        sAct = Sem(es, nc, "sAct")
        sDve = Sem(es, nc, "sDve")
        sPool = Sem(es, nc, "sPool")
        sCc = Sem(es, nc, "sCc")
        sSt = Sem(es, nc, "sSt")
        sPh = Sem(es, nc, "sPh")
        sCst = Sem(es, nc, "sCst")
        sGpC = Sem(es, nc, "sGpC")
        sWprep = Sem(es, nc, "sWprep")
        sX1 = Sem(es, nc, "sX1")
        sX2 = Sem(es, nc, "sX2")
        sYt = Sem(es, nc, "sYt")
        sCs = [Sem(es, nc, "sCs0"), Sem(es, nc, "sCs1")]
        sQs = [Sem(es, nc, "sQs0"), Sem(es, nc, "sQs1")]
        sQst = [Sem(es, nc, "sQst0"), Sem(es, nc, "sQst1")]
        sSh = [Sem(es, nc, "sSh0"), Sem(es, nc, "sSh1")]
        sWst = [Sem(es, nc, "sWst0"), Sem(es, nc, "sWst1"),
                Sem(es, nc, "sWst2"), Sem(es, nc, "sWst3")]
        sQr = [Sem(es, nc, "sQr0"), Sem(es, nc, "sQr1"),
               Sem(es, nc, "sQr2"), Sem(es, nc, "sQr3")]
        sQst2 = [Sem(es, nc, "sQst2_0"), Sem(es, nc, "sQst2_1")]
        sWb = [Sem(es, nc, "sWb0"), Sem(es, nc, "sWb1")]

        block = E(nc.Block())

        selfsem_map = {"tensor": sPe, "scalar": sAct, "vector": sDve,
                       "gpsimd": sPool}

        def sec(name):
            dec = getattr(block, name)
            selfsem = selfsem_map.get(name)

            def wrap(fn):
                def inner(eng):
                    if selfsem is not None and selfsem.n > 0:
                        eng.wait_ge(selfsem.h, selfsem.n)
                    return fn(eng)
                return dec(inner)
            return wrap

        def ch(eng, sem, instr):
            sem.n += 1
            instr.then_inc(sem.h, 1)
            eng.wait_ge(sem.h, sem.n)

        def mm_group(te, items, inc=None):
            n = len(items)
            for i, (o, l, r) in enumerate(items):
                instr = te.matmul(o, l, r, start=(i == 0), stop=(i == n - 1),
                                  skip_group_check=True)
                if inc is not None and i == n - 1:
                    inc.n += 1
                    instr.then_inc(inc.h, 1)
                for _ in range(NOPS):
                    te.nop()

        @sec("sync")
        def _(sy):
            sy.dma_start(out=b1pp[:, :], in_=b1pp_d).then_inc(sCst.h, 16)
            sy.dma_start(out=b2pp[:, :], in_=b2pp_d).then_inc(sCst.h, 16)
            sCst.n += 2 * 16

        @sec("gpsimd")
        def _(gp):
            gp.dma_start(out=bout16[:, :], in_=bout_d).then_inc(sGpC.h, 16)
            gp.dma_start(out=ident16[:, :], in_=iden_d).then_inc(sGpC.h, 16)
            sGpC.n += 32
            gp.memset(ones_c[:, :], 1.0)
            gp.memset(ones_r[:, :], 1.0)
            gp.memset(ones_r16[:, :], 1.0)
            gp.memset(ones2_16[:, :], 1.0).then_inc(sPool.h, 1)
            sPool.n += 1
        pool_consts = sPool.n

        with contextlib.ExitStack() as p1:
            E1 = p1.enter_context
            absrow = E1(nc.sbuf_tensor("absrow", [1, 256], F32))
            tmpA = E1(nc.sbuf_tensor("tmpA", [P, 32], F32))
            Gam = E1(nc.sbuf_tensor("Gam", [P, D], F32))
            Bet = E1(nc.sbuf_tensor("Bet", [P, D], F32))
            binpp = E1(nc.sbuf_tensor("sb_binpp", [P, 48], F32))
            xstage = E1(nc.sbuf_tensor("xstage", [P, D], F32))
            csbuf = E1(nc.sbuf_tensor("csbuf", [P, 2 * 1024], F32))
            utmp = E1(nc.sbuf_tensor("utmp", [P, D], F32))
            kstt1 = E1(nc.sbuf_tensor("kstt1", [P, 1], F32))
            kstt2 = E1(nc.sbuf_tensor("kstt2", [P, 1], F32))
            sstt = E1(nc.sbuf_tensor("sstt", [P, 1], F32))
            xn2b = E1(nc.sbuf_tensor("xn2b", [P, D], BF16))
            xn2T2 = E1(nc.sbuf_tensor("xn2T2", [P, 2 * ND * TAU], BF16))
            wstat = E1(nc.sbuf_tensor("wstat", [P, 4 * ND * 256], BF16))
            KTf = E1(nc.sbuf_tensor("KTf", [P, 16 * TAU], BF16))
            VTf = E1(nc.sbuf_tensor("VTf", [P, 16 * TAU], BF16))
            Ktok2 = E1(nc.sbuf_tensor("Ktok2", [P, 2 * D], BF16))
            Vtok2 = E1(nc.sbuf_tensor("Vtok2", [P, 2 * D], BF16))
            Kre = E1(nc.sbuf_tensor("Kre", [P, 512], F32))
            Kro = E1(nc.sbuf_tensor("Kro", [P, 512], F32))
            rt1 = E1(nc.sbuf_tensor("rt1", [P, 512], F32))
            rt2 = E1(nc.sbuf_tensor("rt2", [P, 512], F32))
            Pbuf = E1(nc.sbuf_tensor("Pbuf", [P, D], BF16))
            qstage = E1(nc.sbuf_tensor("qstage", [P, 2 * 512], BF16))
            qraw = E1(nc.sbuf_tensor("qraw", [P, 2 * 2048], F32))
            ge_b = E1(nc.sbuf_tensor("ge_b", [P, 2048], F8))
            le_b = E1(nc.sbuf_tensor("le_b", [P, 2048], F8))
            absacc = E1(nc.sbuf_tensor("absacc", [P, 2], F32))
            qout = E1(nc.sbuf_tensor("qout", [P, 2 * 2048], BF16))

            wstat_v = wstat[:, :].rearrange("p (b n m) -> p b n m", b=4, n=ND)
            xn2T_v = xn2T2[:, :].rearrange("p (b n t) -> p b n t", b=2, n=ND)
            psT2k = psT[:, 0:2048]
            psT2k_v = psT2k.rearrange("p (n m) -> p n m", n=ND)
            xd_v = x_d.rearrange("(n p) m -> n p m", p=P)
            cos_v = cos_d.rearrange("(n p) m -> n p m", p=P)
            sin_v = sin_d.rearrange("(n p) m -> n p m", p=P)

            @sec("sync")
            def _(sy):
                sy.dma_start(out=Gam[:, :], in_=gam_d).then_inc(sCst.h, 16)
                sy.dma_start(out=Bet[:, :], in_=bet_d).then_inc(sCst.h, 16)
                sy.dma_start(out=binpp[:, :], in_=binpp_d).then_inc(sCst.h, 16)
                sCst.n += 48
            cst_all = sCst.n
            gpc_all = sGpC.n

            wjobs = [("win", jb) for jb in list(range(8, 24)) + list(range(8))] \
                  + [("wout", jb) for jb in range(8)]
            wprep_jb = {}

            conv_sems = [sWprep, sSc]

            @sec("gpsimd")
            def _(eng):
                prev = []
                for ji, (kind, jb) in enumerate(wjobs):
                    csem = conv_sems[ji % 2]
                    if ji >= 2:
                        eng.wait_ge(prev[ji - 2][0].h, prev[ji - 2][1])
                    if kind == "win":
                        eng.dma_start(out=win16s_d[jb],
                                      in_=win_d[:, jb]).then_inc(csem.h, 16)
                    else:
                        eng.dma_start(
                            out=wout16s_d[jb // 2, :, :,
                                          (jb % 2) * 256:(jb % 2) * 256 + 256],
                            in_=wout_d[:, jb]).then_inc(csem.h, 16)
                    csem.n += 16
                    prev.append((csem, csem.n))
                    if kind == "win":
                        wprep_jb[jb] = (csem, csem.n)

            @sec("vector")
            def _(v):
                v.memset(absacc[:, :], 0.0).then_inc(sDve.h, 1)
                sDve.n += 1
            dve_absz = sDve.n

            w1sh_v = w1sh_d.rearrange("(n p) m -> n p m", p=P)
            w2sh_v = w2sh_d.rearrange("(n p) m -> n p m", p=P)
            shard_jobs = [(w1sh_v[i], 1024, 0) for i in range(16)] + \
                         [(w2sh_v[i], 2048, 1) for i in range(8)]

            def emit_shards():
                for i, (src, w, col) in enumerate(shard_jobs):
                    if w == 2048:
                        buf = csbuf[:, 0:2048]
                    else:
                        buf = csbuf[:, (i % 2) * 1024:(i % 2) * 1024 + w]

                    @sec("sync")
                    def _(sy, buf=buf, src=src, i=i, w=w):
                        if i >= 2 or w == 2048:
                            sy.wait_ge(sDve.h, sDve.n)
                        sy.dma_start(out=buf, in_=src).then_inc(sSh[i % 2].h, 16)
                        sSh[i % 2].n += 16
                    shard_sem = sSh[i % 2]
                    shard_ld = shard_sem.n

                    @sec("vector")
                    def _(v, buf=buf, col=col, shard_sem=shard_sem, shard_ld=shard_ld):
                        v.wait_ge(shard_sem.h, shard_ld)
                        v.wait_ge(sDve.h, dve_absz)
                        ch(v, sDve, v.tensor_reduce(sstt[:, :], buf, AXX, ALU.add,
                                                    apply_absolute_value=True))
                        v.tensor_tensor(absacc[:, col:col + 1], absacc[:, col:col + 1],
                                        sstt[:, :], ALU.add).then_inc(sDve.h, 1)
                        sDve.n += 1

            @sec("vector")
            def _(v):
                v.memset(A_acc[:, :], 0.0).then_inc(sDve.h, 1)
                sDve.n += 1
            dve_zeroA = sDve.n

            ev = {
                "psT_act": 0,
                "xn2T_act": [0, 0],
                "x_dve": 0, "x_act": 0, "lnb_pe": 0,
                "wstat_pe": [0, 0],
                "qkv_act": [0, 0, 0, 0, 0],
                "qs_dma": [0, 0],
                "kv_act_done": 0, "ktv_act": 0,
                "kv_p_dve": 0, "kv_p_dve2": 0, "kv_amm_pe": 0,
                "cs_dve": [0, 0],
                "a_dve": dve_zeroA,
            }
            wst_cnt = [0, 0]

            def emit_p1chunk(tc, slot):
                tcl = tc % 4

                @sec("sync")
                def _(sy, tc=tc):
                    if ev["x_dve"]:
                        sy.wait_ge(sDve.h, ev["x_dve"])
                    elif tc < 4:
                        sy.wait_ge(sAct.h, ev["x_act"])
                    sy.dma_start(out=xstage[:, :], in_=xd_v[tc]).then_inc(sX1.h, 16)
                    sX1.n += 16
                x_ld = sX1.n

                @sec("vector")
                def _(v, tc=tc, x_ld=x_ld):
                    v.wait_ge(sX1.h, x_ld)
                    if tc < 4:
                        v.wait_ge(sAct.h, ev["x_act"])
                    v.wait_ge(sDve.h, stats_done[tc // 4])
                    v.wait_ge(sCst.h, cst_all)
                    if ev["lnb_pe"]:
                        v.wait_ge(sPe.h, ev["lnb_pe"])
                    ch(v, sDve, v.scalar_tensor_tensor(utmp[:, :], xstage[:, :],
                                           m_all[:, tc:tc + 1], Gam[:, :],
                                           ALU.subtract, ALU.mult))
                    v.scalar_tensor_tensor(xn2b[:, :], utmp[:, :],
                                           r2_all[:, tc:tc + 1], Bet[:, :],
                                           ALU.mult, ALU.add).then_inc(sDve.h, 1)
                    sDve.n += 1
                ev["x_dve"] = sDve.n

                ev["p1_pend"] = (ev["x_dve"], slot, tcl)

            def emit_p1chunk_tr():
                x_dve, slot, tcl = ev["p1_pend"]

                @sec("tensor")
                def _(te, x_dve=x_dve):
                    te.wait_ge(sDve.h, x_dve)
                    te.wait_ge(sGpC.h, gpc_all)
                    if ev["psT_act"]:
                        te.wait_ge(sAct.h, ev["psT_act"])
                    for dt in range(ND - 1):
                        te.transpose(psT2k[:, dt * P:(dt + 1) * P],
                                     xn2b[:, dt * P:(dt + 1) * P], ident16[:, :])
                    te.transpose(psT2k[:, (ND - 1) * P:ND * P],
                                 xn2b[:, (ND - 1) * P:ND * P],
                                 ident16[:, :]).then_inc(sPe.h, 1)
                    sPe.n += 1
                ev["lnb_pe"] = sPe.n

                @sec("scalar")
                def _(s, slot=slot, tcl=tcl):
                    s.wait_ge(sPe.h, sPe.n)
                    s.activation(xn2T_v[:, slot, :, tcl * P:(tcl + 1) * P],
                                 psT2k_v, AF.Identity).then_inc(sAct.h, 1)
                    sAct.n += 1
                ev["psT_act"] = sAct.n
                if tcl == 3:
                    ev["xn2T_act"][slot] = sAct.n

            def emit_kv_chunk(tau, tcl):
                tc = tau * 4 + tcl
                chalf = tcl % 2
                Ktok = Ktok2[:, chalf * D:(chalf + 1) * D]
                Vtok = Vtok2[:, chalf * D:(chalf + 1) * D]
                kv_p_dve_old = ev["kv_p_dve2"]
                ev["kv_p_dve2"] = ev["kv_p_dve"]

                @sec("sync")
                def _(sy, tc=tc, chalf=chalf):
                    if ev["cs_dve"][chalf]:
                        sy.wait_ge(sDve.h, ev["cs_dve"][chalf])
                    else:
                        sy.wait_ge(sDve.h, dve_shards)
                    sy.dma_start(out=csbuf[:, chalf * 1024:chalf * 1024 + 512],
                                 in_=cos_v[tc]).then_inc(sCs[chalf].h, 16)
                    sy.dma_start(out=csbuf[:, chalf * 1024 + 512:(chalf + 1) * 1024],
                                 in_=sin_v[tc]).then_inc(sCs[chalf].h, 16)
                    sCs[chalf].n += 32
                cs_ld = sCs[chalf].n

                @sec("tensor")
                def _(te, tcl=tcl):
                    te.wait_ge(sGpC.h, gpc_all)
                    te.wait_ge(sAct.h, max(ev["kv_act_done"], ev["psT_act"]))
                    for jj in range(15):
                        te.transpose(psT2k[:, jj * P:(jj + 1) * P],
                                     KTf[:, jj * 512 + tcl * P:jj * 512 + (tcl + 1) * P],
                                     ident16[:, :])
                    te.transpose(psT2k[:, 15 * P:16 * P],
                                 KTf[:, 15 * 512 + tcl * P:15 * 512 + (tcl + 1) * P],
                                 ident16[:, :]).then_inc(sPe.h, 1)
                    sPe.n += 1

                @sec("scalar")
                def _(s, Ktok=Ktok, kv_p_dve_old=kv_p_dve_old):
                    s.wait_ge(sPe.h, sPe.n)
                    if kv_p_dve_old:
                        s.wait_ge(sDve.h, kv_p_dve_old)
                    s.activation(Ktok[:, :], psT2k, AF.Identity).then_inc(sAct.h, 1)
                    sAct.n += 1
                k_tok_act = sAct.n
                ev["psT_act"] = sAct.n

                @sec("tensor")
                def _(te, tcl=tcl, k_tok_act=k_tok_act):
                    te.wait_ge(sAct.h, k_tok_act)
                    for jj in range(15):
                        te.transpose(psT2k[:, jj * P:(jj + 1) * P],
                                     VTf[:, jj * 512 + tcl * P:jj * 512 + (tcl + 1) * P],
                                     ident16[:, :])
                    te.transpose(psT2k[:, 15 * P:16 * P],
                                 VTf[:, 15 * 512 + tcl * P:15 * 512 + (tcl + 1) * P],
                                 ident16[:, :]).then_inc(sPe.h, 1)
                    sPe.n += 1

                @sec("scalar")
                def _(s, Ktok=Ktok, Vtok=Vtok):
                    s.wait_ge(sPe.h, sPe.n)
                    ch(s, sAct, s.activation(Vtok[:, :], psT2k, AF.Identity))
                    ch(s, sAct, s.activation(Pbuf[:, :], Ktok[:, :], AF.Square,
                                             accum_out=kstt1[:, :]))
                    s.activation(Pbuf[:, :], Vtok[:, :], AF.Square,
                                 accum_out=kstt2[:, :]).then_inc(sAct.h, 1)
                    sAct.n += 1
                ev["psT_act"] = sAct.n
                norms_act = sAct.n

                @sec("vector")
                def _(v, norms_act=norms_act):
                    v.wait_ge(sAct.h, norms_act)
                    v.tensor_tensor(kstt1[:, :], kstt1[:, :], kstt2[:, :],
                                    ALU.mult).then_inc(sDve.h, 1)
                    sDve.n += 1

                @sec("scalar")
                def _(s):
                    s.wait_ge(sDve.h, sDve.n)
                    s.sqrt(kstt1[:, :], kstt1[:, :]).then_inc(sAct.h, 1)
                    sAct.n += 1
                nrm_act = sAct.n

                @sec("vector")
                def _(v, cs_ld=cs_ld, chalf=chalf, nrm_act=nrm_act,
                      Ktok=Ktok, Vtok=Vtok):
                    v.wait_ge(sAct.h, nrm_act)
                    v.wait_ge(sCs[chalf].h, cs_ld)
                    if ev["kv_amm_pe"]:
                        v.wait_ge(sPe.h, ev["kv_amm_pe"])
                    ch(v, sDve, v.reciprocal(kstt2[:, :], kstt1[:, :]))
                    cosb = csbuf[:, chalf * 1024:chalf * 1024 + 512]
                    sinb = csbuf[:, chalf * 1024 + 512:(chalf + 1) * 1024]
                    Ke = Ktok[:, 0:1024:2]
                    Ko = Ktok[:, 1:1024:2]
                    Ve = Vtok[:, 0:1024:2]
                    Vo = Vtok[:, 1:1024:2]
                    ch(v, sDve, v.tensor_tensor(rt1[:, :], Ko, sinb, ALU.mult))
                    ch(v, sDve, v.tensor_tensor(Kre[:, :], Ke, cosb, ALU.mult))
                    ch(v, sDve, v.tensor_tensor(Kre[:, :], Kre[:, :], rt1[:, :], ALU.subtract))
                    ch(v, sDve, v.tensor_tensor(rt1[:, :], Ke, sinb, ALU.mult))
                    ch(v, sDve, v.tensor_tensor(Kro[:, :], Ko, cosb, ALU.mult))
                    ch(v, sDve, v.tensor_tensor(Kro[:, :], Kro[:, :], rt1[:, :], ALU.add))
                    ch(v, sDve, v.tensor_tensor(rt1[:, :], Ve, cosb, ALU.mult))
                    ch(v, sDve, v.tensor_tensor(rt2[:, :], Vo, sinb, ALU.mult))
                    ch(v, sDve, v.tensor_tensor(rt1[:, :], rt1[:, :], rt2[:, :], ALU.subtract))
                    ch(v, sDve, v.scalar_tensor_tensor(Pbuf[:, 0:1024:2], Kre[:, :],
                                           kstt2[:, :], rt1[:, :],
                                           ALU.mult, ALU.mult))
                    ch(v, sDve, v.tensor_tensor(rt1[:, :], Vo, cosb, ALU.mult))
                    ch(v, sDve, v.tensor_tensor(rt2[:, :], Ve, sinb, ALU.mult))
                    ch(v, sDve, v.tensor_tensor(rt1[:, :], rt1[:, :], rt2[:, :], ALU.add))
                    ch(v, sDve, v.scalar_tensor_tensor(Pbuf[:, 1:1024:2], Kro[:, :],
                                           kstt2[:, :], rt1[:, :],
                                           ALU.mult, ALU.mult))
                    v.scalar_tensor_tensor(Pbuf[:, 1024:2048],
                                           Ktok[:, 1024:2048], kstt2[:, :],
                                           Vtok[:, 1024:2048],
                                           ALU.mult, ALU.mult).then_inc(sDve.h, 1)
                    sDve.n += 1
                ev["cs_dve"][chalf] = sDve.n
                ev["kv_p_dve"] = sDve.n

            def emit_kv_back():
                p_dve = ev["kv_p_dve"]

                @sec("tensor")
                def _(te, p_dve=p_dve):
                    te.wait_ge(sDve.h, p_dve)
                    te.wait_ge(sPool.h, pool_consts)
                    te.wait_ge(sAct.h, ev["ktv_act"])
                    for jj in range(15):
                        te.matmul(psA[:, 2560 + jj * 2:2560 + (jj + 1) * 2],
                                  Pbuf[:, jj * P:(jj + 1) * P],
                                  ones2_16[:, :], start=True, stop=True,
                                  skip_group_check=True)
                    te.matmul(psA[:, 2560 + 15 * 2:2560 + 16 * 2],
                              Pbuf[:, 15 * P:16 * P],
                              ones2_16[:, :], start=True, stop=True,
                              skip_group_check=True).then_inc(sPe.h, 1)
                    sPe.n += 1
                ev["kv_amm_pe"] = sPe.n

                @sec("scalar")
                def _(s):
                    s.wait_ge(sPe.h, sPe.n)
                    s.activation(tmpA[:, :], psA[:, 2560:2592],
                                 AF.Identity).then_inc(sAct.h, 1)
                    sAct.n += 1
                ev["ktv_act"] = sAct.n

                @sec("vector")
                def _(v):
                    v.wait_ge(sAct.h, ev["ktv_act"])
                    v.wait_ge(sDve.h, ev["a_dve"])
                    v.tensor_tensor(A_acc[:, :], A_acc[:, :],
                                    tmpA[:, 0:32:2], ALU.add).then_inc(sDve.h, 1)
                    sDve.n += 1
                ev["a_dve"] = sDve.n

            w1t_v = w1t_d.rearrange("(n p) m -> n p m", p=P)
            w2t_v = w2t_d.rearrange("(n p) m -> n p m", p=P)
            quant_jobs = []
            for rc in range(16):
                for ph in range(8):
                    quant_jobs.append(("w1", rc, ph))
            for rc in range(64):
                for ph in range(2):
                    quant_jobs.append(("w2", rc, ph))
            NQ = len(quant_jobs)
            qstate = {"i": 0, "pool_after": [0] * NQ, "st_after": [0] * NQ,
                      "sc_after": [0] * NQ, "dve_after": [0] * NQ}

            def emit_quant_piece():
                i = qstate["i"]
                if i >= NQ:
                    return
                qstate["i"] += 1
                kind, rc, ph = quant_jobs[i]
                col = 0 if kind == "w1" else 1
                rawb = qraw[:, (i % 4) * 1024:(i % 4) * 1024 + 1024]
                outb = qout[:, (i % 4) * 1024:(i % 4) * 1024 + 1024]
                geb = ge_b[:, (i % 2) * 1024:(i % 2) * 1024 + 1024]
                leb = le_b[:, (i % 2) * 1024:(i % 2) * 1024 + 1024]
                src = w1t_v[rc, :, ph * 1024:(ph + 1) * 1024] if kind == "w1" \
                    else w2t_v[rc, :, ph * 1024:(ph + 1) * 1024]

                @sec("scalar")
                def _(s, i=i, src=src, rawb=rawb):
                    if i >= 4:
                        s.wait_ge(sPool.h, qstate["pool_after"][i - 4])
                    s.dma_start(out=rawb, in_=src).then_inc(sQr[i % 4].h, 16)
                    sQr[i % 4].n += 16
                    qstate["sc_after"][i] = sQr[i % 4].n

                @sec("vector")
                def _(v, i=i, col=col, rawb=rawb, geb=geb, leb=leb):
                    v.wait_ge(sQr[i % 4].h, qstate["sc_after"][i])
                    v.wait_ge(sDve.h, dve_thresh)
                    if i >= 2:
                        v.wait_ge(sPool.h, qstate["pool_after"][i - 2])
                    ch(v, sDve, v.tensor_scalar(geb, rawb,
                                                thpos[:, col:col + 1], None, ALU.is_ge))
                    v.tensor_scalar(leb, rawb, thneg[:, col:col + 1], None,
                                    ALU.is_le).then_inc(sDve.h, 1)
                    sDve.n += 1
                    qstate["dve_after"][i] = sDve.n

                @sec("gpsimd")
                def _(gp, i=i, outb=outb, geb=geb, leb=leb):
                    gp.wait_ge(sDve.h, qstate["dve_after"][i])
                    if i >= 4:
                        _qs, _qn = qstate["st_after"][i - 4]
                        gp.wait_ge(_qs.h, _qn)
                    gp.tensor_tensor(outb, leb, geb,
                                     ALU.subtract).then_inc(sPool.h, 1)
                    sPool.n += 1
                    qstate["pool_after"][i] = sPool.n

                @sec("sync")
                def _(sy, i=i, kind=kind, rc=rc, ph=ph, outb=outb):
                    sy.wait_ge(sPool.h, qstate["pool_after"][i])
                    if kind == "w1":
                        dst = w1q_d[ph * 2:(ph + 1) * 2, :, rc, :].rearrange(
                            "f p m -> p f m")
                        srcv = outb.rearrange("p (f m) -> p f m", f=2)
                    else:
                        dst = w2q_d[ph * 8:(ph + 1) * 8, :, rc, :].rearrange(
                            "e p m -> p e m")
                        srcv = outb.rearrange("p (e m) -> p e m", e=8)
                    qsem = sQst[i % 2] if i % 4 < 2 else sQst2[i % 2]
                    sy.dma_start(out=dst, in_=srcv).then_inc(qsem.h, 16)
                    qsem.n += 16
                    qstate["st_after"][i] = (qsem, qsem.n)

            stats_done = [0, 0, 0, 0]
            for tc in range(NT):
                sbuf_x = xstage if tc % 2 == 0 else utmp
                xsem = sX1 if tc % 2 == 0 else sLd

                @sec("sync")
                def _(sy, tc=tc, sbuf_x=sbuf_x, xsem=xsem):
                    if tc >= 2:
                        sy.wait_ge(sAct.h, ev["x_act"] - 1)
                    sy.dma_start(out=sbuf_x[:, :], in_=xd_v[tc]).then_inc(xsem.h, 16)
                    xsem.n += 16
                x_ld = xsem.n

                @sec("scalar")
                def _(s, x_ld=x_ld, sbuf_x=sbuf_x, xsem=xsem):
                    s.wait_ge(xsem.h, x_ld)
                    s.wait_ge(sDve.h, sDve.n)
                    ch(s, sAct, s.activation(xn2b[:, :], sbuf_x[:, :], AF.Copy,
                                             accum_out=stt1[:, :]))
                    s.activation(xn2b[:, :], sbuf_x[:, :], AF.Square,
                                 accum_out=stt2[:, :]).then_inc(sAct.h, 1)
                    sAct.n += 1
                ev["x_act"] = sAct.n

                @sec("vector")
                def _(v, tc=tc):
                    v.wait_ge(sAct.h, ev["x_act"])
                    ch(v, sDve, v.tensor_scalar(m_all[:, tc:tc + 1], stt1[:, :],
                                                1.0 / D, None, ALU.mult))
                    ch(v, sDve, v.tensor_scalar(stt2[:, :], stt2[:, :], 1.0 / D,
                                                None, ALU.mult))
                    ch(v, sDve, v.tensor_tensor(stt1[:, :], m_all[:, tc:tc + 1],
                                                m_all[:, tc:tc + 1], ALU.mult))
                    ch(v, sDve, v.tensor_tensor(r1_all[:, tc:tc + 1], stt2[:, :],
                                                stt1[:, :], ALU.subtract))
                    v.tensor_scalar(r2_all[:, tc:tc + 1], r1_all[:, tc:tc + 1],
                                    1.0 + EPS, EPS * EPS,
                                    ALU.mult, ALU.add).then_inc(sDve.h, 1)
                    sDve.n += 1

                if tc % 4 == 3:
                    b0 = (tc // 4) * 4

                    @sec("vector")
                    def _(v, b0=b0):
                        v.tensor_scalar(r1_all[:, b0:b0 + 4], r1_all[:, b0:b0 + 4],
                                        1.0, EPS, ALU.mult,
                                        ALU.add).then_inc(sDve.h, 1)
                        sDve.n += 1

                    @sec("scalar")
                    def _(s, b0=b0):
                        s.wait_ge(sDve.h, sDve.n)
                        ch(s, sAct, s.sqrt(r1_all[:, b0:b0 + 4],
                                           r1_all[:, b0:b0 + 4]))
                        s.sqrt(r2_all[:, b0:b0 + 4],
                               r2_all[:, b0:b0 + 4]).then_inc(sAct.h, 1)
                        sAct.n += 1
                    fin_act = sAct.n

                    @sec("vector")
                    def _(v, b0=b0, fin_act=fin_act):
                        v.wait_ge(sAct.h, fin_act)
                        ch(v, sDve, v.reciprocal(r1_all[:, b0:b0 + 4],
                                                 r1_all[:, b0:b0 + 4]))
                        v.reciprocal(r2_all[:, b0:b0 + 4],
                                     r2_all[:, b0:b0 + 4]).then_inc(sDve.h, 1)
                        sDve.n += 1
                    stats_done[tc // 4] = sDve.n

            for tcl0 in range(4):
                emit_p1chunk(tcl0, 0)
                emit_p1chunk_tr()
            emit_shards()
            dve_shards = sDve.n

            @sec("gpsimd")
            def _(gp):
                gp.wait_ge(sDve.h, dve_shards)
                gp.dma_start(out=absd.rearrange("a (p m) -> p (a m)", p=P),
                             in_=absacc[:, :]).then_inc(sGp.h, 16)
                sGp.n += 16
                gp.wait_ge(sGp.h, sGp.n)
                gp.dma_start(out=absrow[:, :], in_=absd).then_inc(sGp.h, 16)
                sGp.n += 16
            gp_absrow = sGp.n

            @sec("vector")
            def _(v):
                v.wait_ge(sGp.h, gp_absrow)
                ch(v, sDve, v.tensor_reduce(scrow[0:1, 0:1], absrow[0:1, 0:256:2],
                                            AXX, ALU.add))
                v.tensor_reduce(scrow[0:1, 1:2], absrow[0:1, 1:256:2], AXX,
                                ALU.add).then_inc(sDve.h, 1)
                sDve.n += 1
            dve_scrow = sDve.n

            @sec("gpsimd")
            def _(gp):
                gp.wait_ge(sDve.h, dve_scrow)
                gp.dma_start(out=ccs_i, in_=scrow[:, :]).then_inc(sGp.h, 16)
                sGp.n += 16
                gp.wait_ge(sGp.h, sGp.n)
                gp.collective_compute(
                    "AllReduce", ALU.add, replica_groups=[list(range(8))],
                    ins=[ccs_i.opt()], outs=[ccs_o.opt()],
                ).then_inc(sCc.h, 1)
                sCc.n += 1
                gp.wait_ge(sCc.h, sCc.n)
                gp.dma_start(out=thpos[:, :],
                             in_=ccs_o.partition_broadcast(P)).then_inc(sGp.h, 16)
                sGp.n += 16
            gp_scales = sGp.n

            @sec("vector")
            def _(v):
                v.wait_ge(sGp.h, gp_scales)
                ch(v, sDve, v.tensor_scalar(thpos[:, :], thpos[:, :], 1.0 / NW, 1e-8,
                                            ALU.mult, ALU.max))
                ch(v, sDve, v.tensor_scalar(thneg[:, :], thpos[:, :], -0.5, None,
                                            ALU.mult))
                v.tensor_scalar(thpos[:, :], thpos[:, :], 0.5, None,
                                ALU.mult).then_inc(sDve.h, 1)
                sDve.n += 1
            dve_thresh = sDve.n

            tile_order = list(range(16, 48)) + list(range(16))

            def jb_of_blk(b):
                return tile_order[2 * b] // 2

            wld_map = {}

            def emit_wstat_load(ltau, lblk):
                wsem = sWst[lblk % 4]

                @sec("scalar")
                def _(s, ltau=ltau, lblk=lblk, wsem=wsem):
                    jb = jb_of_blk(lblk)
                    if ltau == 0:
                        csem, ccnt = wprep_jb[jb]
                        s.wait_ge(csem.h, ccnt)
                    s.dma_start(out=wstat_v[:, lblk % 4],
                                in_=win16s_d[jb]).then_inc(wsem.h, 16)
                    wsem.n += 16
                wld_map[(ltau, lblk)] = (wsem, wsem.n)

            for bb in range(3):
                emit_wstat_load(0, bb)

            for tau in range(NTAU):
                slot = tau % 2
                tile_idx = 0

                for blk in range(24):
                    half = blk % 4

                    for sub in range(2):
                        j = tile_order[2 * blk + sub]
                        bank = tile_idx % 5
                        tile_idx += 1
                        wsem, wld = wld_map[(tau, blk)]

                        @sec("tensor")
                        def _(te, j=j, sub=sub, half=half, bank=bank, slot=slot,
                              wsem=wsem, wld=wld, blk=blk, tau=tau):
                            te.wait_ge(wsem.h, wld)
                            te.wait_ge(sAct.h, ev["xn2T_act"][slot])
                            if ev["qkv_act"][bank]:
                                te.wait_ge(sAct.h, ev["qkv_act"][bank])
                            group = [(psA[:, bank * 512:(bank + 1) * 512],
                                      wstat_v[:, half, d, sub * P:(sub + 1) * P],
                                      xn2T_v[:, slot, d, :]) for d in range(ND)]
                            mm_group(te, group, inc=sPe)
                        ev["wstat_pe"][0] = sPe.n

                        if sub == 1:
                            if blk + 3 < 24:
                                emit_wstat_load(tau, blk + 3)
                            elif tau + 1 < NTAU:
                                emit_wstat_load(tau + 1, blk + 3 - 24)

                        @sec("scalar")
                        def _(s, j=j, bank=bank):
                            s.wait_ge(sPe.h, sPe.n)
                            s.wait_ge(sCst.h, cst_all)
                            src = psA[:, bank * 512:(bank + 1) * 512]
                            if j < 16:
                                qslot = j % 2
                                if ev["qs_dma"][qslot]:
                                    s.wait_ge(sQs[qslot].h, ev["qs_dma"][qslot])
                                instr = s.activation(
                                    qstage[:, qslot * 512:(qslot + 1) * 512], src,
                                    AF.Identity, bias=binpp[:, j:j + 1])
                            elif j < 32:
                                instr = s.activation(
                                    KTf[:, (j - 16) * 512:(j - 15) * 512], src,
                                    AF.Identity, bias=binpp[:, j:j + 1])
                            else:
                                instr = s.activation(
                                    VTf[:, (j - 32) * 512:(j - 31) * 512], src,
                                    AF.Identity, bias=binpp[:, j:j + 1])
                            instr.then_inc(sAct.h, 1)
                            sAct.n += 1
                        ev["qkv_act"][bank] = sAct.n

                        if j < 16:
                            qslot = j % 2

                            @sec("sync")
                            def _(sy, j=j, tau=tau, qslot=qslot):
                                sy.wait_ge(sAct.h, sAct.n)
                                sy.dma_start(
                                    out=qts_d[tau, :, j, :],
                                    in_=qstage[:, qslot * 512:(qslot + 1) * 512],
                                ).then_inc(sQs[qslot].h, 16)
                                sQs[qslot].n += 16
                            ev["qs_dma"][qslot] = sQs[qslot].n

                    if blk == 15:
                        ev["kv_act_done"] = sAct.n
                    if blk == 1 and tau > 0:
                        emit_kv_back()
                    if blk in (4, 9, 14, 19) and tau + 1 < NTAU:
                        emit_p1chunk((tau + 1) * 4 + (4, 9, 14, 19).index(blk),
                                     (tau + 1) % 2)
                    if blk in (6, 11, 16, 21) and tau + 1 < NTAU:
                        emit_p1chunk_tr()
                    if blk in (19, 21, 23):
                        emit_kv_back()
                    if blk in (17, 19, 21, 23):
                        emit_kv_chunk(tau, (blk - 17) // 2)
                    emit_quant_piece()
                    emit_quant_piece()
                    if blk % 3 != 0:
                        emit_quant_piece()

            emit_kv_back()
            while qstate["i"] < NQ:
                emit_quant_piece()
            quant_w1_done = [qstate["st_after"][126], qstate["st_after"][127]]
            quant_all_done = [qstate["st_after"][254], qstate["st_after"][255]]

            a_dve_done = sDve.n

            @sec("gpsimd")
            def _(gp):
                gp.wait_ge(sDve.h, a_dve_done)
                gp.dma_start(out=cca_i, in_=A_acc[:, :]).then_inc(sGp.h, 16)
                sGp.n += 16
                gp.wait_ge(sGp.h, sGp.n)
                gp.collective_compute(
                    "AllReduce", ALU.add,
                    replica_groups=[[0, 1], [2, 3], [4, 5], [6, 7]],
                    ins=[cca_i.opt()], outs=[cca_o.opt()],
                ).then_inc(sCc.h, 1)
                sCc.n += 1
                gp.wait_ge(sCc.h, sCc.n)
                gp.dma_start(out=A_one[:, :], in_=cca_o).then_inc(sGp.h, 16)
                sGp.n += 16
            gp_A_done = sGp.n

            @sec("sync")
            def _(sy):
                for sem in [sX1, sLd, sCs[0], sCs[1], sQs[0], sQs[1], sQst[0],
                            sQst[1], sQst2[0], sQst2[1], sSh[0], sSh[1], sCst]:
                    if sem.n:
                        sy.wait_ge(sem.h, sem.n)
                sy.nop().then_inc(sPh.h, 1)
                sPh.n += 1

            @sec("scalar")
            def _(s):
                for sem in sWst + sQr + [sSc]:
                    if sem.n:
                        s.wait_ge(sem.h, sem.n)
                s.nop().then_inc(sPh.h, 1)
                sPh.n += 1

            @sec("gpsimd")
            def _(gp):
                for sem in [sGp, sGpC, sWprep]:
                    if sem.n:
                        gp.wait_ge(sem.h, sem.n)
                gp.nop().then_inc(sPh.h, 1)
                sPh.n += 1

            @sec("vector")
            def _(v):
                v.nop().then_inc(sPh.h, 1)
                sPh.n += 1

            @sec("tensor")
            def _(te):
                for sem in sWst:
                    if sem.n:
                        te.wait_ge(sem.h, sem.n)
                te.nop().then_inc(sPh.h, 1)
                sPh.n += 1

        with contextlib.ExitStack() as p2:
            E2 = p2.enter_context
            hT = E2(nc.sbuf_tensor("hT", [P, NF * 512], BF16))
            xnT = E2(nc.sbuf_tensor("xnT", [P, ND * 512], BF16))
            YT = E2(nc.sbuf_tensor("YT", [P, ND * 512], BF16))
            gT = E2(nc.sbuf_tensor("gT", [P, 16 * 512], BF16))
            gtok4 = E2(nc.sbuf_tensor("gtok4", [P, 4 * D], BF16))
            wbuf = E2(nc.sbuf_tensor("wbuf", [P, 2 * NF * P], BF16))
            xstage2 = E2(nc.sbuf_tensor("xstage2", [P, D], F32))
            xnb2 = E2(nc.sbuf_tensor("xnb2", [P, D], BF16))
            buf1 = E2(nc.sbuf_tensor("buf1", [P, D], F32))
            s_tile = E2(nc.sbuf_tensor("s_tile", [P, 4 * D], BF16))

            wbuf_v = wbuf[:, :].rearrange("p (b c) -> p b c", b=2)
            xnT_v = xnT[:, :].rearrange("p (n t) -> p n t", n=ND)
            YT_v = YT[:, :].rearrange("p (n t) -> p n t", n=ND)
            hT_v = hT[:, :].rearrange("p (n t) -> p n t", n=NF)
            out_v = out_d.rearrange("(n p) m -> n p m", p=P)

            for eng in ["tensor", "scalar", "sync", "gpsimd", "vector"]:
                @sec(eng)
                def _(e):
                    e.wait_ge(sPh.h, 5)

            NBANK = 3
            ev2 = {
                "xs_dve": 0,
                "prep_pe": 0,
                "psT_act": 0,
                "xnT_act": 0,
                "wbuf_pe": [0, 0],
                "bank_act": [0] * NBANK,
                "yt_pe": 0,
                "yt_dve": 0,
                "gtok_act": 0,
                "comb_dve": 0,
                "s_act": 0,
                "out_st": 0,
                "bank": 0,
            }

            def nextbank():
                b = ev2["bank"]
                ev2["bank"] = (b + 1) % NBANK
                return b

            def emit_prep_chunk(tau, tcl):
                tc = tau * 4 + tcl

                @sec("sync")
                def _(sy, tc=tc):
                    if ev2["xs_dve"]:
                        sy.wait_ge(sDve.h, ev2["xs_dve"])
                    sy.dma_start(out=xstage2[:, :], in_=xd_v[tc]).then_inc(sX2.h, 16)
                    sX2.n += 16
                x_ld = sX2.n

                @sec("vector")
                def _(v, tc=tc, x_ld=x_ld):
                    v.wait_ge(sX2.h, x_ld)
                    if ev2["prep_pe"]:
                        v.wait_ge(sPe.h, ev2["prep_pe"])
                    v.tensor_scalar(xnb2[:, :], xstage2[:, :], m_all[:, tc:tc + 1],
                                    r1_all[:, tc:tc + 1], ALU.subtract,
                                    ALU.mult).then_inc(sDve.h, 1)
                    sDve.n += 1
                ev2["xs_dve"] = sDve.n
                ln_dve = sDve.n

                @sec("tensor")
                def _(te, ln_dve=ln_dve):
                    te.wait_ge(sDve.h, ln_dve)
                    te.wait_ge(sGpC.h, gpc_all)
                    if ev2["psT_act"]:
                        te.wait_ge(sAct.h, ev2["psT_act"])
                    for dt in range(ND - 1):
                        te.transpose(psT2k[:, dt * P:(dt + 1) * P],
                                     xnb2[:, dt * P:(dt + 1) * P], ident16[:, :])
                    te.transpose(psT2k[:, (ND - 1) * P:ND * P],
                                 xnb2[:, (ND - 1) * P:ND * P],
                                 ident16[:, :]).then_inc(sPe.h, 1)
                    sPe.n += 1
                ev2["prep_pe"] = sPe.n

                @sec("scalar")
                def _(s, tcl=tcl):
                    s.wait_ge(sPe.h, sPe.n)
                    s.copy(xnT_v[:, :, tcl * P:(tcl + 1) * P], psT2k_v).then_inc(sAct.h, 1)
                    sAct.n += 1
                ev2["psT_act"] = sAct.n
                if tcl == 3:
                    ev2["xnT_act"] = sAct.n

            def emit_ytp(tau):
                @sec("sync")
                def _(sy, tau=tau):
                    if ev2["yt_pe"]:
                        sy.wait_ge(sPe.h, ev2["yt_pe"])
                    sy.dma_start(out=YT_v, in_=qts_d[tau]).then_inc(sYt.h, 16)
                    sYt.n += 16
                qt_ld = sYt.n

                @sec("vector")
                def _(v, qt_ld=qt_ld):
                    v.wait_ge(sYt.h, qt_ld)
                    v.wait_ge(sDve.h, aone_dve)
                    for dt in range(ND - 1):
                        v.tensor_scalar(YT_v[:, dt, :], YT_v[:, dt, :],
                                        A_one[:, dt:dt + 1], None, ALU.mult)
                    v.tensor_scalar(YT_v[:, ND - 1, :], YT_v[:, ND - 1, :],
                                    A_one[:, ND - 1:ND], None,
                                    ALU.mult).then_inc(sDve.h, 1)
                    sDve.n += 1
                ev2["yt_dve"] = sDve.n

            def emit_f1(tau):
                for fb in range(16):
                    whalf = fb % 2

                    @sec("gpsimd")
                    def _(gp, fb=fb, whalf=whalf, tau=tau):
                        if ev2["wbuf_pe"][whalf]:
                            gp.wait_ge(sPe.h, ev2["wbuf_pe"][whalf])
                        if tau == 0 and fb < 2:
                            for _sem, _val in quant_w1_done:
                                gp.wait_ge(_sem.h, _val)
                        gp.dma_start(
                            out=wbuf_v[:, whalf, 0:ND * 512].rearrange("p (n m) -> p n m", n=ND),
                            in_=w1q_d[fb]).then_inc(sWb[whalf].h, 16)
                        sWb[whalf].n += 16
                    w_ld = sWb[whalf].n

                    for sub in range(4):
                        ft = fb * 4 + sub
                        bank = nextbank()

                        @sec("tensor")
                        def _(te, sub=sub, whalf=whalf, bank=bank, w_ld=w_ld):
                            te.wait_ge(sWb[whalf].h, w_ld)
                            te.wait_ge(sAct.h, ev2["xnT_act"])
                            if ev2["bank_act"][bank]:
                                te.wait_ge(sAct.h, ev2["bank_act"][bank])
                            group = [(psA[:, bank * 512:(bank + 1) * 512],
                                      wbuf_v[:, whalf, d * 512 + sub * P:d * 512 + (sub + 1) * P],
                                      xnT_v[:, d, :]) for d in range(ND)]
                            mm_group(te, group, inc=sPe)
                        ev2["wbuf_pe"][whalf] = sPe.n

                        @sec("scalar")
                        def _(s, ft=ft, bank=bank):
                            s.wait_ge(sPe.h, sPe.n)
                            s.wait_ge(sCst.h, cst_all)
                            s.activation(hT_v[:, ft, :],
                                         psA[:, bank * 512:(bank + 1) * 512],
                                         AF.Gelu, bias=b1pp[:, ft:ft + 1],
                                         scale=-1.0).then_inc(sAct.h, 1)
                            sAct.n += 1
                        ev2["bank_act"][bank] = sAct.n

            def emit_f2(tau, prep_tau):
                f1_done = sAct.n
                for et in range(16):
                    whalf = et % 2
                    bank = nextbank()

                    @sec("gpsimd")
                    def _(gp, et=et, whalf=whalf, tau=tau):
                        gp.wait_ge(sPe.h, ev2["wbuf_pe"][whalf])
                        if tau == 0 and et < 2:
                            for _sem, _val in quant_all_done:
                                gp.wait_ge(_sem.h, _val)
                        gp.dma_start(
                            out=wbuf_v[:, whalf, :].rearrange("p (n m) -> p n m", n=NF),
                            in_=w2q_d[et]).then_inc(sWb[whalf].h, 16)
                        sWb[whalf].n += 16
                    w_ld = sWb[whalf].n

                    @sec("tensor")
                    def _(te, et=et, whalf=whalf, bank=bank, w_ld=w_ld, f1_done=f1_done):
                        te.wait_ge(sWb[whalf].h, w_ld)
                        if et == 0:
                            te.wait_ge(sAct.h, f1_done)
                        if ev2["bank_act"][bank]:
                            te.wait_ge(sAct.h, ev2["bank_act"][bank])
                        group = [(psA[:, bank * 512:(bank + 1) * 512],
                                  wbuf_v[:, whalf, f * P:(f + 1) * P],
                                  hT_v[:, f, :]) for f in range(NF)]
                        mm_group(te, group, inc=sPe)
                    ev2["wbuf_pe"][whalf] = sPe.n

                    @sec("scalar")
                    def _(s, et=et, bank=bank):
                        s.wait_ge(sPe.h, sPe.n)
                        s.wait_ge(sCst.h, cst_all)
                        s.activation(gT[:, et * 512:(et + 1) * 512],
                                     psA[:, bank * 512:(bank + 1) * 512],
                                     AF.Silu, bias=b2pp[:, et:et + 1],
                                     scale=-1.0).then_inc(sAct.h, 1)
                        sAct.n += 1
                    ev2["bank_act"][bank] = sAct.n

                    if prep_tau is not None and et in (2, 5, 8, 11):
                        emit_prep_chunk(prep_tau, (2, 5, 8, 11).index(et))

            def emit_gtr(tau):
                f2_done = sAct.n
                for tcl in range(4):
                    @sec("tensor")
                    def _(te, tcl=tcl, f2_done=f2_done):
                        if tcl == 0:
                            te.wait_ge(sAct.h, f2_done)
                        if ev2["psT_act"]:
                            te.wait_ge(sAct.h, ev2["psT_act"])
                        for et in range(15):
                            te.transpose(psT2k[:, et * P:(et + 1) * P],
                                         gT[:, et * 512 + tcl * P:et * 512 + (tcl + 1) * P],
                                         ident16[:, :])
                        te.transpose(psT2k[:, 15 * P:16 * P],
                                     gT[:, 15 * 512 + tcl * P:15 * 512 + (tcl + 1) * P],
                                     ident16[:, :]).then_inc(sPe.h, 1)
                        sPe.n += 1

                    @sec("scalar")
                    def _(s, tcl=tcl):
                        s.wait_ge(sPe.h, sPe.n)
                        if ev2["comb_dve"]:
                            s.wait_ge(sDve.h, ev2["comb_dve"])
                        s.copy(gtok4[:, tcl * D:(tcl + 1) * D], psT2k).then_inc(sAct.h, 1)
                        sAct.n += 1
                    ev2["psT_act"] = sAct.n
                ev2["gtok_act"] = sAct.n

            def emit_w(tau):
                for et in range(4):
                    whalf = et % 2

                    @sec("gpsimd")
                    def _(gp, et=et, whalf=whalf):
                        gp.wait_ge(sPe.h, ev2["wbuf_pe"][whalf])
                        gp.dma_start(
                            out=wbuf_v[:, whalf, 0:ND * 512].rearrange("p (n m) -> p n m", n=ND),
                            in_=wout16s_d[et]).then_inc(sWb[whalf].h, 16)
                        sWb[whalf].n += 16
                    w_ld = sWb[whalf].n

                    for tcl in range(4):
                        bank = nextbank()

                        @sec("tensor")
                        def _(te, et=et, tcl=tcl, whalf=whalf, bank=bank, w_ld=w_ld):
                            te.wait_ge(sWb[whalf].h, w_ld)
                            te.wait_ge(sGpC.h, gpc_all)
                            te.wait_ge(sPool.h, pool_consts)
                            te.wait_ge(sDve.h, ev2["yt_dve"])
                            if ev2["bank_act"][bank]:
                                te.wait_ge(sAct.h, ev2["bank_act"][bank])
                            group = [(psA[:, bank * 512:(bank + 1) * 512],
                                      YT_v[:, d, tcl * P:(tcl + 1) * P],
                                      wbuf_v[:, whalf, d * 512:(d + 1) * 512])
                                     for d in range(ND)]
                            group.append((psA[:, bank * 512:(bank + 1) * 512],
                                          ones_r16[:, :],
                                          bout16[:, et * 512:(et + 1) * 512]))
                            mm_group(te, group, inc=sPe)
                        ev2["wbuf_pe"][whalf] = sPe.n

                        @sec("scalar")
                        def _(s, et=et, tcl=tcl, bank=bank):
                            s.wait_ge(sPe.h, sPe.n)
                            if ev2["comb_dve"]:
                                s.wait_ge(sDve.h, ev2["comb_dve"])
                            s.copy(s_tile[:, tcl * D + et * 512:tcl * D + (et + 1) * 512],
                                   psA[:, bank * 512:(bank + 1) * 512]).then_inc(sAct.h, 1)
                            sAct.n += 1
                        ev2["bank_act"][bank] = sAct.n
                ev2["yt_pe"] = sPe.n
                ev2["s_act"] = sAct.n

            def emit_comb(tau):
                for tcl in range(4):
                    tc = tau * 4 + tcl

                    @sec("sync")
                    def _(sy, tc=tc):
                        if ev2["xs_dve"]:
                            sy.wait_ge(sDve.h, ev2["xs_dve"])
                        if ev2["out_st"]:
                            sy.wait_ge(sSt.h, ev2["out_st"])
                        sy.dma_start(out=xstage2[:, :], in_=xd_v[tc]).then_inc(sX2.h, 16)
                        sX2.n += 16
                    x_ld = sX2.n

                    @sec("vector")
                    def _(v, tc=tc, tcl=tcl, x_ld=x_ld):
                        v.wait_ge(sX2.h, x_ld)
                        v.wait_ge(sAct.h, max(ev2["gtok_act"], ev2["s_act"]))
                        ch(v, sDve, v.tensor_scalar(buf1[:, :], xstage2[:, :],
                                        m_all[:, tc:tc + 1],
                                        r1_all[:, tc:tc + 1], ALU.subtract, ALU.mult))
                        ch(v, sDve, v.tensor_tensor(buf1[:, :], buf1[:, :],
                                        s_tile[:, tcl * D:(tcl + 1) * D], ALU.add))
                        ch(v, sDve, v.tensor_tensor(buf1[:, :], buf1[:, :],
                                        gtok4[:, tcl * D:(tcl + 1) * D], ALU.mult))
                        v.tensor_tensor(xstage2[:, :], buf1[:, :], xstage2[:, :],
                                        ALU.add).then_inc(sDve.h, 1)
                        sDve.n += 1
                    ev2["xs_dve"] = sDve.n

                    @sec("sync")
                    def _(sy, tc=tc):
                        sy.wait_ge(sDve.h, sDve.n)
                        sy.dma_start(out=out_v[tc], in_=xstage2[:, :]).then_inc(sSt.h, 16)
                        sSt.n += 16
                    ev2["out_st"] = sSt.n
                ev2["comb_dve"] = sDve.n

            emit_prep_chunk(0, 0)
            emit_prep_chunk(0, 1)
            emit_prep_chunk(0, 2)
            emit_prep_chunk(0, 3)

            @sec("vector")
            def _(v):
                v.wait_ge(sGp.h, gp_A_done)
                v.tensor_scalar(A_one[:, :], A_one[:, :], 0.5, 0.5,
                                ALU.mult, ALU.add).then_inc(sDve.h, 1)
                sDve.n += 1
            aone_dve = sDve.n

            emit_ytp(0)
            for tau in range(NTAU):
                emit_f1(tau)
                emit_f2(tau, tau + 1 if tau + 1 < NTAU else None)
                emit_gtr(tau)
                emit_w(tau)
                if tau + 1 < NTAU:
                    emit_ytp(tau + 1)
                emit_comb(tau)

            @sec("sync")
            def _(sy):
                sy.wait_ge(sSt.h, sSt.n)

            @sec("gpsimd")
            def _(gp):
                gp.wait_ge(sSt.h, sSt.n)

    return nc


_NC_CACHE = {}


def kernel(L0_attn_X, L0_attn_W_in, L0_attn_b_in, L0_attn_W_out, L0_attn_b_out,
           L0_attn_gamma, L0_attn_beta, L0_ffn_w1, L0_ffn_b1, L0_ffn_w2, L0_ffn_b2):
    X = np.ascontiguousarray(np.asarray(L0_attn_X, np.float32))
    W_in = np.ascontiguousarray(np.asarray(L0_attn_W_in, np.float32))
    b_in = np.asarray(L0_attn_b_in, np.float32).reshape(-1)
    W_out = np.ascontiguousarray(np.asarray(L0_attn_W_out, np.float32))
    b_out = np.asarray(L0_attn_b_out, np.float32).reshape(-1)
    gamma = np.asarray(L0_attn_gamma, np.float32).reshape(-1)
    beta = np.asarray(L0_attn_beta, np.float32).reshape(-1)
    w1 = np.asarray(L0_ffn_w1, np.float32)
    b1 = np.asarray(L0_ffn_b1, np.float32).reshape(-1)
    w2 = np.asarray(L0_ffn_w2, np.float32)
    b2 = np.asarray(L0_ffn_b2, np.float32).reshape(-1)

    w1t = np.ascontiguousarray(w1.T)
    w2t = np.ascontiguousarray(w2.T)
    win_s = np.ascontiguousarray(
        W_in.reshape(16, P, 24, 256).transpose(1, 2, 0, 3))
    wout_s = np.ascontiguousarray(
        W_out.reshape(16, P, 8, 256).transpose(1, 2, 0, 3))
    binpp = np.ascontiguousarray(b_in.reshape(48, P).T)
    b1pp = np.ascontiguousarray(b1.reshape(64, P).T)
    b2pp = np.ascontiguousarray(b2.reshape(16, P).T)
    iden = np.eye(P, dtype=np.float32)
    inv_freq = 1.0 / (10000.0 ** (np.arange(0, 1024, 2, dtype=np.float64) / 1024.0))

    if "nc" not in _NC_CACHE:
        _NC_CACHE["nc"] = build_nc()
    nc = _NC_CACHE["nc"]

    in_maps = []
    for c in range(8):
        b, h = c // 2, c % 2
        pos = np.arange(h * T, (h + 1) * T, dtype=np.float64)
        freqs = pos[:, None] * inv_freq[None, :]
        in_maps.append(dict(
            x=np.ascontiguousarray(X[b, h * T:(h + 1) * T, :]),
            w_in=win_s, w_out=wout_s, w1t=w1t, w2t=w2t,
            w1sh=np.ascontiguousarray(w1t[:, c * 1024:(c + 1) * 1024]),
            w2sh=np.ascontiguousarray(w2t[c * 1024:(c + 1) * 1024, :]),
            gam=np.ascontiguousarray(np.broadcast_to(gamma.reshape(1, D), (P, D))),
            bet=np.ascontiguousarray(np.broadcast_to(beta.reshape(1, D), (P, D))),
            b1pp=b1pp, b2pp=b2pp, bout=b_out.reshape(1, D),
            binpp=binpp,
            cosT=np.cos(freqs).astype(np.float32),
            sinT=np.sin(freqs).astype(np.float32),
            iden=iden,
        ))

    res = run_bass_kernel_spmd(nc, in_maps, core_ids=list(range(8)))
    out = np.empty((4, 4096, D), np.float32)
    for c in range(8):
        b, h = c // 2, c % 2
        out[b, h * T:(h + 1) * T, :] = res.results[c]["out"]
    return out


# version 24 (final)
# speedup vs baseline: 1.0603x; 1.0603x over previous
import contextlib
import numpy as np
import concourse.bass as bass
import concourse.mybir as mybir
from concourse.bass_utils import run_bass_kernel_spmd

F32 = mybir.dt.float32
BF16 = mybir.dt.bfloat16
F8 = mybir.dt.float8e4
AF = mybir.ActivationFunctionType
ALU = mybir.AluOpType

P = 128
T = 2048
D = 2048
F = 8192
NT = T // P
ND = D // P
NF = F // P
TAU = 512
NTAU = T // TAU
EPS = 1e-5
NW = float(F * D)
NOPS = 0
AXX = mybir.AxisListType.X
STAGE = 99
QKV_BIAS = True


class Sem:
    def __init__(self, es, nc, name):
        self.h = es.enter_context(nc.semaphore(name))
        self.n = 0


def build_nc():
    nc = bass.Bass("TRN2", target_bir_lowering=False)

    ap = lambda t: t.ap()
    x_d = ap(nc.dram_tensor("x", [T, D], F32, kind="ExternalInput"))
    win_d = ap(nc.dram_tensor("w_in", [P, 24, ND, 256], F32, kind="ExternalInput"))
    wout_d = ap(nc.dram_tensor("w_out", [P, 8, ND, 256], F32, kind="ExternalInput"))
    w1t_d = ap(nc.dram_tensor("w1t", [D, F], F32, kind="ExternalInput"))
    w2t_d = ap(nc.dram_tensor("w2t", [F, D], F32, kind="ExternalInput"))
    w1sh_d = ap(nc.dram_tensor("w1sh", [D, F // 8], F32, kind="ExternalInput"))
    w2sh_d = ap(nc.dram_tensor("w2sh", [F // 8, D], F32, kind="ExternalInput"))
    gam_d = ap(nc.dram_tensor("gam", [P, D], F32, kind="ExternalInput"))
    bet_d = ap(nc.dram_tensor("bet", [P, D], F32, kind="ExternalInput"))
    b1pp_d = ap(nc.dram_tensor("b1pp", [P, 64], F32, kind="ExternalInput"))
    b2pp_d = ap(nc.dram_tensor("b2pp", [P, 16], F32, kind="ExternalInput"))
    bout_d = ap(nc.dram_tensor("bout", [1, D], F32, kind="ExternalInput"))
    binpp_d = ap(nc.dram_tensor("binpp", [P, 48], F32, kind="ExternalInput"))
    cos_d = ap(nc.dram_tensor("cosT", [T, 512], F32, kind="ExternalInput"))
    sin_d = ap(nc.dram_tensor("sinT", [T, 512], F32, kind="ExternalInput"))
    iden_d = ap(nc.dram_tensor("iden", [P, P], F32, kind="ExternalInput"))
    out_d = ap(nc.dram_tensor("out", [T, D], F32, kind="ExternalOutput"))
    win16s_d = ap(nc.dram_tensor("win16s", [24, P, ND, 256], BF16, kind="Internal"))
    wout16s_d = ap(nc.dram_tensor("wout16s", [4, P, ND, 512], BF16, kind="Internal"))
    w1q_d = ap(nc.dram_tensor("w1q", [16, P, ND, 512], BF16, kind="Internal"))
    w2q_d = ap(nc.dram_tensor("w2q", [16, P, NF, P], BF16, kind="Internal"))
    qts_d = ap(nc.dram_tensor("qtd", [4, P, ND, 512], BF16, kind="Internal"))
    ccs_i = ap(nc.dram_tensor("ccs_i", [1, 2], F32, kind="Internal"))
    ccs_o = ap(nc.dram_tensor("ccs_o", [1, 2], F32, kind="Internal"))
    cca_i = ap(nc.dram_tensor("cca_i", [P, 16], F32, kind="Internal"))
    cca_o = ap(nc.dram_tensor("cca_o", [P, 16], F32, kind="Internal"))
    absd = ap(nc.dram_tensor("absd", [1, 256], F32, kind="Internal"))

    with contextlib.ExitStack() as es:
        E = es.enter_context
        m_all = E(nc.sbuf_tensor("m_all", [P, NT], F32))
        r1_all = E(nc.sbuf_tensor("r1_all", [P, NT], F32))
        r2_all = E(nc.sbuf_tensor("r2_all", [P, NT], F32))
        A_acc = E(nc.sbuf_tensor("A_acc", [P, 16], F32))
        A_one = E(nc.sbuf_tensor("A_one", [P, 16], F32))
        ident16 = E(nc.sbuf_tensor("ident16", [P, P], BF16))
        ones_c = E(nc.sbuf_tensor("ones_c", [P, 1], F32))
        ones_r = E(nc.sbuf_tensor("ones_r", [1, P], F32))
        ones_r16 = E(nc.sbuf_tensor("ones_r16", [1, P], BF16))
        ones2_16 = E(nc.sbuf_tensor("ones2_16", [P, 2], BF16))
        b1pp = E(nc.sbuf_tensor("sb_b1pp", [P, 64], F32))
        b2pp = E(nc.sbuf_tensor("sb_b2pp", [P, 16], F32))
        bout16 = E(nc.sbuf_tensor("bout16", [1, D], BF16))
        scrow = E(nc.sbuf_tensor("scrow", [1, 2], F32))
        thpos = E(nc.sbuf_tensor("thpos", [P, 2], F32))
        thneg = E(nc.sbuf_tensor("thneg", [P, 2], F32))
        stt1 = E(nc.sbuf_tensor("stt1", [P, 1], F32))
        stt2 = E(nc.sbuf_tensor("stt2", [P, 1], F32))
        psA = E(nc.psum_tensor("psA", [P, 3072], F32))
        psT = E(nc.psum_tensor("psT", [P, 2048], BF16))

        sLd = Sem(es, nc, "sLd")
        sSc = Sem(es, nc, "sSc")
        sGp = Sem(es, nc, "sGp")
        sPe = Sem(es, nc, "sPe")
        sAct = Sem(es, nc, "sAct")
        sDve = Sem(es, nc, "sDve")
        sPool = Sem(es, nc, "sPool")
        sCc = Sem(es, nc, "sCc")
        sSt = Sem(es, nc, "sSt")
        sPh = Sem(es, nc, "sPh")
        sCst = Sem(es, nc, "sCst")
        sGpC = Sem(es, nc, "sGpC")
        sWprep = Sem(es, nc, "sWprep")
        sX1 = Sem(es, nc, "sX1")
        sX2 = Sem(es, nc, "sX2")
        sYt = Sem(es, nc, "sYt")
        sCs = [Sem(es, nc, "sCs0"), Sem(es, nc, "sCs1")]
        sQs = [Sem(es, nc, "sQs0"), Sem(es, nc, "sQs1")]
        sQst = [Sem(es, nc, "sQst0"), Sem(es, nc, "sQst1")]
        sSh = [Sem(es, nc, "sSh0"), Sem(es, nc, "sSh1")]
        sWst = [Sem(es, nc, "sWst0"), Sem(es, nc, "sWst1"),
                Sem(es, nc, "sWst2"), Sem(es, nc, "sWst3")]
        sQr = [Sem(es, nc, "sQr0"), Sem(es, nc, "sQr1"),
               Sem(es, nc, "sQr2"), Sem(es, nc, "sQr3")]
        sQst2 = [Sem(es, nc, "sQst2_0"), Sem(es, nc, "sQst2_1")]
        sWb = [Sem(es, nc, "sWb0"), Sem(es, nc, "sWb1")]

        block = E(nc.Block())

        selfsem_map = {"tensor": sPe, "scalar": sAct, "vector": sDve,
                       "gpsimd": sPool}

        def sec(name):
            dec = getattr(block, name)
            selfsem = selfsem_map.get(name)

            def wrap(fn):
                def inner(eng):
                    if selfsem is not None and selfsem.n > 0:
                        eng.wait_ge(selfsem.h, selfsem.n)
                    return fn(eng)
                return dec(inner)
            return wrap

        def ch(eng, sem, instr):
            sem.n += 1
            instr.then_inc(sem.h, 1)
            eng.wait_ge(sem.h, sem.n)

        def mm_group(te, items, inc=None):
            n = len(items)
            for i, (o, l, r) in enumerate(items):
                instr = te.matmul(o, l, r, start=(i == 0), stop=(i == n - 1),
                                  skip_group_check=True)
                if inc is not None and i == n - 1:
                    inc.n += 1
                    instr.then_inc(inc.h, 1)
                for _ in range(NOPS):
                    te.nop()

        @sec("sync")
        def _(sy):
            sy.dma_start(out=b1pp[:, :], in_=b1pp_d).then_inc(sCst.h, 16)
            sy.dma_start(out=b2pp[:, :], in_=b2pp_d).then_inc(sCst.h, 16)
            sCst.n += 2 * 16

        @sec("gpsimd")
        def _(gp):
            gp.dma_start(out=bout16[:, :], in_=bout_d).then_inc(sGpC.h, 16)
            gp.dma_start(out=ident16[:, :], in_=iden_d).then_inc(sGpC.h, 16)
            sGpC.n += 32
            gp.memset(ones_c[:, :], 1.0)
            gp.memset(ones_r[:, :], 1.0)
            gp.memset(ones_r16[:, :], 1.0)
            gp.memset(ones2_16[:, :], 1.0).then_inc(sPool.h, 1)
            sPool.n += 1
        pool_consts = sPool.n

        with contextlib.ExitStack() as p1:
            E1 = p1.enter_context
            absrow = E1(nc.sbuf_tensor("absrow", [1, 256], F32))
            tmpA = E1(nc.sbuf_tensor("tmpA", [P, 32], F32))
            Gam = E1(nc.sbuf_tensor("Gam", [P, D], F32))
            Bet = E1(nc.sbuf_tensor("Bet", [P, D], F32))
            binpp = E1(nc.sbuf_tensor("sb_binpp", [P, 48], F32))
            xstage = E1(nc.sbuf_tensor("xstage", [P, D], F32))
            csbuf = E1(nc.sbuf_tensor("csbuf", [P, 2 * 1024], F32))
            utmp = E1(nc.sbuf_tensor("utmp", [P, D], F32))
            kstt1 = E1(nc.sbuf_tensor("kstt1", [P, 1], F32))
            kstt2 = E1(nc.sbuf_tensor("kstt2", [P, 1], F32))
            sstt = E1(nc.sbuf_tensor("sstt", [P, 1], F32))
            xn2b = E1(nc.sbuf_tensor("xn2b", [P, D], BF16))
            xn2T2 = E1(nc.sbuf_tensor("xn2T2", [P, 2 * ND * TAU], BF16))
            wstat = E1(nc.sbuf_tensor("wstat", [P, 4 * ND * 256], BF16))
            KTf = E1(nc.sbuf_tensor("KTf", [P, 16 * TAU], BF16))
            VTf = E1(nc.sbuf_tensor("VTf", [P, 16 * TAU], BF16))
            Ktok2 = E1(nc.sbuf_tensor("Ktok2", [P, 2 * D], BF16))
            Vtok2 = E1(nc.sbuf_tensor("Vtok2", [P, 2 * D], BF16))
            Kre = E1(nc.sbuf_tensor("Kre", [P, 512], F32))
            Kro = E1(nc.sbuf_tensor("Kro", [P, 512], F32))
            rt1 = E1(nc.sbuf_tensor("rt1", [P, 512], F32))
            rt2 = E1(nc.sbuf_tensor("rt2", [P, 512], F32))
            Pbuf = E1(nc.sbuf_tensor("Pbuf", [P, D], BF16))
            qstage = E1(nc.sbuf_tensor("qstage", [P, 2 * 512], BF16))
            qraw = E1(nc.sbuf_tensor("qraw", [P, 2 * 2048], F32))
            ge_b = E1(nc.sbuf_tensor("ge_b", [P, 2048], F8))
            le_b = E1(nc.sbuf_tensor("le_b", [P, 2048], F8))
            absacc = E1(nc.sbuf_tensor("absacc", [P, 2], F32))
            qout = E1(nc.sbuf_tensor("qout", [P, 2 * 2048], BF16))

            wstat_v = wstat[:, :].rearrange("p (b n m) -> p b n m", b=4, n=ND)
            xn2T_v = xn2T2[:, :].rearrange("p (b n t) -> p b n t", b=2, n=ND)
            psT2k = psT[:, 0:2048]
            psT2k_v = psT2k.rearrange("p (n m) -> p n m", n=ND)
            xd_v = x_d.rearrange("(n p) m -> n p m", p=P)
            cos_v = cos_d.rearrange("(n p) m -> n p m", p=P)
            sin_v = sin_d.rearrange("(n p) m -> n p m", p=P)

            @sec("sync")
            def _(sy):
                sy.dma_start(out=Gam[:, :], in_=gam_d).then_inc(sCst.h, 16)
                sy.dma_start(out=Bet[:, :], in_=bet_d).then_inc(sCst.h, 16)
                sy.dma_start(out=binpp[:, :], in_=binpp_d).then_inc(sCst.h, 16)
                sCst.n += 48
            cst_all = sCst.n
            gpc_all = sGpC.n

            wjobs = [("win", jb) for jb in list(range(8, 24)) + list(range(8))] \
                  + [("wout", jb) for jb in range(8)]
            wprep_jb = {}

            conv_sems = [sWprep, sSc]

            @sec("gpsimd")
            def _(eng):
                prev = []
                for ji, (kind, jb) in enumerate(wjobs):
                    csem = conv_sems[ji % 2]
                    if ji >= 2:
                        eng.wait_ge(prev[ji - 2][0].h, prev[ji - 2][1])
                    if kind == "win":
                        eng.dma_start(out=win16s_d[jb],
                                      in_=win_d[:, jb]).then_inc(csem.h, 16)
                    else:
                        eng.dma_start(
                            out=wout16s_d[jb // 2, :, :,
                                          (jb % 2) * 256:(jb % 2) * 256 + 256],
                            in_=wout_d[:, jb]).then_inc(csem.h, 16)
                    csem.n += 16
                    prev.append((csem, csem.n))
                    if kind == "win":
                        wprep_jb[jb] = (csem, csem.n)

            @sec("vector")
            def _(v):
                v.memset(absacc[:, :], 0.0).then_inc(sDve.h, 1)
                sDve.n += 1
            dve_absz = sDve.n

            w1sh_v = w1sh_d.rearrange("(n p) m -> n p m", p=P)
            w2sh_v = w2sh_d.rearrange("(n p) m -> n p m", p=P)
            shard_jobs = [(w1sh_v[i], 1024, 0) for i in range(16)] + \
                         [(w2sh_v[i], 2048, 1) for i in range(8)]

            def emit_shards():
                for i, (src, w, col) in enumerate(shard_jobs):
                    if w == 2048:
                        buf = csbuf[:, 0:2048]
                    else:
                        buf = csbuf[:, (i % 2) * 1024:(i % 2) * 1024 + w]

                    @sec("sync")
                    def _(sy, buf=buf, src=src, i=i, w=w):
                        if i >= 2 or w == 2048:
                            sy.wait_ge(sDve.h, sDve.n)
                        sy.dma_start(out=buf, in_=src).then_inc(sSh[i % 2].h, 16)
                        sSh[i % 2].n += 16
                    shard_sem = sSh[i % 2]
                    shard_ld = shard_sem.n

                    @sec("vector")
                    def _(v, buf=buf, col=col, shard_sem=shard_sem, shard_ld=shard_ld):
                        v.wait_ge(shard_sem.h, shard_ld)
                        v.wait_ge(sDve.h, dve_absz)
                        ch(v, sDve, v.tensor_reduce(sstt[:, :], buf, AXX, ALU.add,
                                                    apply_absolute_value=True))
                        v.tensor_tensor(absacc[:, col:col + 1], absacc[:, col:col + 1],
                                        sstt[:, :], ALU.add).then_inc(sDve.h, 1)
                        sDve.n += 1

            @sec("vector")
            def _(v):
                v.memset(A_acc[:, :], 0.0).then_inc(sDve.h, 1)
                sDve.n += 1
            dve_zeroA = sDve.n

            ev = {
                "psT_act": 0,
                "xn2T_act": [0, 0],
                "x_dve": 0, "x_act": 0, "lnb_pe": 0,
                "wstat_pe": [0, 0],
                "qkv_act": [0, 0, 0, 0, 0],
                "qs_dma": [0, 0],
                "kv_act_done": 0, "ktv_act": 0,
                "kv_p_dve": 0, "kv_p_dve2": 0, "kv_amm_pe": 0,
                "cs_dve": [0, 0],
                "a_dve": dve_zeroA,
            }
            wst_cnt = [0, 0]

            def emit_p1chunk(tc, slot):
                tcl = tc % 4

                @sec("sync")
                def _(sy, tc=tc):
                    if ev["x_dve"]:
                        sy.wait_ge(sDve.h, ev["x_dve"])
                    elif tc < 4:
                        sy.wait_ge(sAct.h, ev["x_act"])
                    sy.dma_start(out=xstage[:, :], in_=xd_v[tc]).then_inc(sX1.h, 16)
                    sX1.n += 16
                x_ld = sX1.n

                @sec("vector")
                def _(v, tc=tc, x_ld=x_ld):
                    v.wait_ge(sX1.h, x_ld)
                    if tc < 4:
                        v.wait_ge(sAct.h, ev["x_act"])
                    v.wait_ge(sDve.h, stats_done[tc // 4])
                    v.wait_ge(sCst.h, cst_all)
                    if ev["lnb_pe"]:
                        v.wait_ge(sPe.h, ev["lnb_pe"])
                    ch(v, sDve, v.scalar_tensor_tensor(utmp[:, :], xstage[:, :],
                                           m_all[:, tc:tc + 1], Gam[:, :],
                                           ALU.subtract, ALU.mult))
                    v.scalar_tensor_tensor(xn2b[:, :], utmp[:, :],
                                           r2_all[:, tc:tc + 1], Bet[:, :],
                                           ALU.mult, ALU.add).then_inc(sDve.h, 1)
                    sDve.n += 1
                ev["x_dve"] = sDve.n

                ev["p1_pend"] = (ev["x_dve"], slot, tcl)

            def emit_p1chunk_tr():
                x_dve, slot, tcl = ev["p1_pend"]

                @sec("tensor")
                def _(te, x_dve=x_dve):
                    te.wait_ge(sDve.h, x_dve)
                    te.wait_ge(sGpC.h, gpc_all)
                    if ev["psT_act"]:
                        te.wait_ge(sAct.h, ev["psT_act"])
                    for dt in range(ND - 1):
                        te.transpose(psT2k[:, dt * P:(dt + 1) * P],
                                     xn2b[:, dt * P:(dt + 1) * P], ident16[:, :])
                    te.transpose(psT2k[:, (ND - 1) * P:ND * P],
                                 xn2b[:, (ND - 1) * P:ND * P],
                                 ident16[:, :]).then_inc(sPe.h, 1)
                    sPe.n += 1
                ev["lnb_pe"] = sPe.n

                @sec("scalar")
                def _(s, slot=slot, tcl=tcl):
                    s.wait_ge(sPe.h, sPe.n)
                    s.activation(xn2T_v[:, slot, :, tcl * P:(tcl + 1) * P],
                                 psT2k_v, AF.Identity).then_inc(sAct.h, 1)
                    sAct.n += 1
                ev["psT_act"] = sAct.n
                if tcl == 3:
                    ev["xn2T_act"][slot] = sAct.n

            def emit_kv_chunk(tau, tcl):
                tc = tau * 4 + tcl
                chalf = tcl % 2
                Ktok = Ktok2[:, chalf * D:(chalf + 1) * D]
                Vtok = Vtok2[:, chalf * D:(chalf + 1) * D]
                kv_p_dve_old = ev["kv_p_dve2"]
                ev["kv_p_dve2"] = ev["kv_p_dve"]

                @sec("sync")
                def _(sy, tc=tc, chalf=chalf):
                    if ev["cs_dve"][chalf]:
                        sy.wait_ge(sDve.h, ev["cs_dve"][chalf])
                    else:
                        sy.wait_ge(sDve.h, dve_shards)
                    sy.dma_start(out=csbuf[:, chalf * 1024:chalf * 1024 + 512],
                                 in_=cos_v[tc]).then_inc(sCs[chalf].h, 16)
                    sy.dma_start(out=csbuf[:, chalf * 1024 + 512:(chalf + 1) * 1024],
                                 in_=sin_v[tc]).then_inc(sCs[chalf].h, 16)
                    sCs[chalf].n += 32
                cs_ld = sCs[chalf].n

                @sec("tensor")
                def _(te, tcl=tcl):
                    te.wait_ge(sGpC.h, gpc_all)
                    te.wait_ge(sAct.h, max(ev["kv_act_done"], ev["psT_act"]))
                    for jj in range(15):
                        te.transpose(psT2k[:, jj * P:(jj + 1) * P],
                                     KTf[:, jj * 512 + tcl * P:jj * 512 + (tcl + 1) * P],
                                     ident16[:, :])
                    te.transpose(psT2k[:, 15 * P:16 * P],
                                 KTf[:, 15 * 512 + tcl * P:15 * 512 + (tcl + 1) * P],
                                 ident16[:, :]).then_inc(sPe.h, 1)
                    sPe.n += 1

                @sec("scalar")
                def _(s, Ktok=Ktok, kv_p_dve_old=kv_p_dve_old):
                    s.wait_ge(sPe.h, sPe.n)
                    if kv_p_dve_old:
                        s.wait_ge(sDve.h, kv_p_dve_old)
                    s.activation(Ktok[:, :], psT2k, AF.Identity).then_inc(sAct.h, 1)
                    sAct.n += 1
                k_tok_act = sAct.n
                ev["psT_act"] = sAct.n

                @sec("tensor")
                def _(te, tcl=tcl, k_tok_act=k_tok_act):
                    te.wait_ge(sAct.h, k_tok_act)
                    for jj in range(15):
                        te.transpose(psT2k[:, jj * P:(jj + 1) * P],
                                     VTf[:, jj * 512 + tcl * P:jj * 512 + (tcl + 1) * P],
                                     ident16[:, :])
                    te.transpose(psT2k[:, 15 * P:16 * P],
                                 VTf[:, 15 * 512 + tcl * P:15 * 512 + (tcl + 1) * P],
                                 ident16[:, :]).then_inc(sPe.h, 1)
                    sPe.n += 1

                @sec("scalar")
                def _(s, Ktok=Ktok, Vtok=Vtok):
                    s.wait_ge(sPe.h, sPe.n)
                    ch(s, sAct, s.activation(Vtok[:, :], psT2k, AF.Identity))
                    ch(s, sAct, s.activation(Pbuf[:, :], Ktok[:, :], AF.Square,
                                             accum_out=kstt1[:, :]))
                    s.activation(Pbuf[:, :], Vtok[:, :], AF.Square,
                                 accum_out=kstt2[:, :]).then_inc(sAct.h, 1)
                    sAct.n += 1
                ev["psT_act"] = sAct.n
                norms_act = sAct.n

                @sec("vector")
                def _(v, norms_act=norms_act):
                    v.wait_ge(sAct.h, norms_act)
                    v.tensor_tensor(kstt1[:, :], kstt1[:, :], kstt2[:, :],
                                    ALU.mult).then_inc(sDve.h, 1)
                    sDve.n += 1

                @sec("scalar")
                def _(s):
                    s.wait_ge(sDve.h, sDve.n)
                    s.sqrt(kstt1[:, :], kstt1[:, :]).then_inc(sAct.h, 1)
                    sAct.n += 1
                nrm_act = sAct.n

                @sec("vector")
                def _(v, cs_ld=cs_ld, chalf=chalf, nrm_act=nrm_act,
                      Ktok=Ktok, Vtok=Vtok):
                    v.wait_ge(sAct.h, nrm_act)
                    v.wait_ge(sCs[chalf].h, cs_ld)
                    if ev["kv_amm_pe"]:
                        v.wait_ge(sPe.h, ev["kv_amm_pe"])
                    ch(v, sDve, v.reciprocal(kstt2[:, :], kstt1[:, :]))
                    cosb = csbuf[:, chalf * 1024:chalf * 1024 + 512]
                    sinb = csbuf[:, chalf * 1024 + 512:(chalf + 1) * 1024]
                    Ke = Ktok[:, 0:1024:2]
                    Ko = Ktok[:, 1:1024:2]
                    Ve = Vtok[:, 0:1024:2]
                    Vo = Vtok[:, 1:1024:2]
                    ch(v, sDve, v.tensor_tensor(rt1[:, :], Ko, sinb, ALU.mult))
                    ch(v, sDve, v.tensor_tensor(Kre[:, :], Ke, cosb, ALU.mult))
                    ch(v, sDve, v.tensor_tensor(Kre[:, :], Kre[:, :], rt1[:, :], ALU.subtract))
                    ch(v, sDve, v.tensor_tensor(rt1[:, :], Ke, sinb, ALU.mult))
                    ch(v, sDve, v.tensor_tensor(Kro[:, :], Ko, cosb, ALU.mult))
                    ch(v, sDve, v.tensor_tensor(Kro[:, :], Kro[:, :], rt1[:, :], ALU.add))
                    ch(v, sDve, v.tensor_tensor(rt1[:, :], Ve, cosb, ALU.mult))
                    ch(v, sDve, v.tensor_tensor(rt2[:, :], Vo, sinb, ALU.mult))
                    ch(v, sDve, v.tensor_tensor(rt1[:, :], rt1[:, :], rt2[:, :], ALU.subtract))
                    ch(v, sDve, v.scalar_tensor_tensor(Pbuf[:, 0:1024:2], Kre[:, :],
                                           kstt2[:, :], rt1[:, :],
                                           ALU.mult, ALU.mult))
                    ch(v, sDve, v.tensor_tensor(rt1[:, :], Vo, cosb, ALU.mult))
                    ch(v, sDve, v.tensor_tensor(rt2[:, :], Ve, sinb, ALU.mult))
                    ch(v, sDve, v.tensor_tensor(rt1[:, :], rt1[:, :], rt2[:, :], ALU.add))
                    ch(v, sDve, v.scalar_tensor_tensor(Pbuf[:, 1:1024:2], Kro[:, :],
                                           kstt2[:, :], rt1[:, :],
                                           ALU.mult, ALU.mult))
                    v.scalar_tensor_tensor(Pbuf[:, 1024:2048],
                                           Ktok[:, 1024:2048], kstt2[:, :],
                                           Vtok[:, 1024:2048],
                                           ALU.mult, ALU.mult).then_inc(sDve.h, 1)
                    sDve.n += 1
                ev["cs_dve"][chalf] = sDve.n
                ev["kv_p_dve"] = sDve.n

            def emit_kv_back():
                p_dve = ev["kv_p_dve"]

                @sec("tensor")
                def _(te, p_dve=p_dve):
                    te.wait_ge(sDve.h, p_dve)
                    te.wait_ge(sPool.h, pool_consts)
                    te.wait_ge(sAct.h, ev["ktv_act"])
                    for jj in range(15):
                        te.matmul(psA[:, 2560 + jj * 2:2560 + (jj + 1) * 2],
                                  Pbuf[:, jj * P:(jj + 1) * P],
                                  ones2_16[:, :], start=True, stop=True,
                                  skip_group_check=True)
                    te.matmul(psA[:, 2560 + 15 * 2:2560 + 16 * 2],
                              Pbuf[:, 15 * P:16 * P],
                              ones2_16[:, :], start=True, stop=True,
                              skip_group_check=True).then_inc(sPe.h, 1)
                    sPe.n += 1
                ev["kv_amm_pe"] = sPe.n

                @sec("scalar")
                def _(s):
                    s.wait_ge(sPe.h, sPe.n)
                    s.activation(tmpA[:, :], psA[:, 2560:2592],
                                 AF.Identity).then_inc(sAct.h, 1)
                    sAct.n += 1
                ev["ktv_act"] = sAct.n

                @sec("vector")
                def _(v):
                    v.wait_ge(sAct.h, ev["ktv_act"])
                    v.wait_ge(sDve.h, ev["a_dve"])
                    v.tensor_tensor(A_acc[:, :], A_acc[:, :],
                                    tmpA[:, 0:32:2], ALU.add).then_inc(sDve.h, 1)
                    sDve.n += 1
                ev["a_dve"] = sDve.n

            w1t_v = w1t_d.rearrange("(n p) m -> n p m", p=P)
            w2t_v = w2t_d.rearrange("(n p) m -> n p m", p=P)
            quant_jobs = []
            for rc in range(16):
                for ph in range(8):
                    quant_jobs.append(("w1", rc, ph))
            for rc in range(64):
                for ph in range(2):
                    quant_jobs.append(("w2", rc, ph))
            NQ = len(quant_jobs)
            qstate = {"i": 0, "pool_after": [0] * NQ, "st_after": [0] * NQ,
                      "sc_after": [0] * NQ, "dve_after": [0] * NQ}

            def emit_quant_piece():
                i = qstate["i"]
                if i >= NQ:
                    return
                qstate["i"] += 1
                kind, rc, ph = quant_jobs[i]
                col = 0 if kind == "w1" else 1
                rawb = qraw[:, (i % 4) * 1024:(i % 4) * 1024 + 1024]
                outb = qout[:, (i % 4) * 1024:(i % 4) * 1024 + 1024]
                geb = ge_b[:, (i % 2) * 1024:(i % 2) * 1024 + 1024]
                leb = le_b[:, (i % 2) * 1024:(i % 2) * 1024 + 1024]
                src = w1t_v[rc, :, ph * 1024:(ph + 1) * 1024] if kind == "w1" \
                    else w2t_v[rc, :, ph * 1024:(ph + 1) * 1024]

                @sec("scalar")
                def _(s, i=i, src=src, rawb=rawb):
                    if i >= 4:
                        s.wait_ge(sPool.h, qstate["pool_after"][i - 4])
                    s.dma_start(out=rawb, in_=src).then_inc(sQr[i % 4].h, 16)
                    sQr[i % 4].n += 16
                    qstate["sc_after"][i] = sQr[i % 4].n

                @sec("vector")
                def _(v, i=i, col=col, rawb=rawb, geb=geb, leb=leb):
                    v.wait_ge(sQr[i % 4].h, qstate["sc_after"][i])
                    v.wait_ge(sDve.h, dve_thresh)
                    if i >= 2:
                        v.wait_ge(sPool.h, qstate["pool_after"][i - 2])
                    ch(v, sDve, v.tensor_scalar(geb, rawb,
                                                thpos[:, col:col + 1], None, ALU.is_ge))
                    v.tensor_scalar(leb, rawb, thneg[:, col:col + 1], None,
                                    ALU.is_le).then_inc(sDve.h, 1)
                    sDve.n += 1
                    qstate["dve_after"][i] = sDve.n

                @sec("gpsimd")
                def _(gp, i=i, outb=outb, geb=geb, leb=leb):
                    gp.wait_ge(sDve.h, qstate["dve_after"][i])
                    if i >= 4:
                        _qs, _qn = qstate["st_after"][i - 4]
                        gp.wait_ge(_qs.h, _qn)
                    gp.tensor_tensor(outb, leb, geb,
                                     ALU.subtract).then_inc(sPool.h, 1)
                    sPool.n += 1
                    qstate["pool_after"][i] = sPool.n

                @sec("sync")
                def _(sy, i=i, kind=kind, rc=rc, ph=ph, outb=outb):
                    sy.wait_ge(sPool.h, qstate["pool_after"][i])
                    if kind == "w1":
                        dst = w1q_d[ph * 2:(ph + 1) * 2, :, rc, :].rearrange(
                            "f p m -> p f m")
                        srcv = outb.rearrange("p (f m) -> p f m", f=2)
                    else:
                        dst = w2q_d[ph * 8:(ph + 1) * 8, :, rc, :].rearrange(
                            "e p m -> p e m")
                        srcv = outb.rearrange("p (e m) -> p e m", e=8)
                    qsem = sQst[i % 2] if i % 4 < 2 else sQst2[i % 2]
                    sy.dma_start(out=dst, in_=srcv).then_inc(qsem.h, 16)
                    qsem.n += 16
                    qstate["st_after"][i] = (qsem, qsem.n)

            stats_done = [0, 0, 0, 0]
            for tc in range(NT):
                sbuf_x = xstage if tc % 2 == 0 else utmp
                xsem = sX1 if tc % 2 == 0 else sLd

                @sec("sync")
                def _(sy, tc=tc, sbuf_x=sbuf_x, xsem=xsem):
                    if tc >= 2:
                        sy.wait_ge(sAct.h, ev["x_act"] - 1)
                    sy.dma_start(out=sbuf_x[:, :], in_=xd_v[tc]).then_inc(xsem.h, 16)
                    xsem.n += 16
                x_ld = xsem.n

                @sec("scalar")
                def _(s, x_ld=x_ld, sbuf_x=sbuf_x, xsem=xsem):
                    s.wait_ge(xsem.h, x_ld)
                    s.wait_ge(sDve.h, sDve.n)
                    ch(s, sAct, s.activation(xn2b[:, :], sbuf_x[:, :], AF.Copy,
                                             accum_out=stt1[:, :]))
                    s.activation(xn2b[:, :], sbuf_x[:, :], AF.Square,
                                 accum_out=stt2[:, :]).then_inc(sAct.h, 1)
                    sAct.n += 1
                ev["x_act"] = sAct.n

                @sec("vector")
                def _(v, tc=tc):
                    v.wait_ge(sAct.h, ev["x_act"])
                    ch(v, sDve, v.tensor_scalar(m_all[:, tc:tc + 1], stt1[:, :],
                                                1.0 / D, None, ALU.mult))
                    ch(v, sDve, v.tensor_scalar(stt2[:, :], stt2[:, :], 1.0 / D,
                                                None, ALU.mult))
                    ch(v, sDve, v.tensor_tensor(stt1[:, :], m_all[:, tc:tc + 1],
                                                m_all[:, tc:tc + 1], ALU.mult))
                    ch(v, sDve, v.tensor_tensor(r1_all[:, tc:tc + 1], stt2[:, :],
                                                stt1[:, :], ALU.subtract))
                    v.tensor_scalar(r2_all[:, tc:tc + 1], r1_all[:, tc:tc + 1],
                                    1.0 + EPS, EPS * EPS,
                                    ALU.mult, ALU.add).then_inc(sDve.h, 1)
                    sDve.n += 1

                if tc % 4 == 3:
                    b0 = (tc // 4) * 4

                    @sec("vector")
                    def _(v, b0=b0):
                        v.tensor_scalar(r1_all[:, b0:b0 + 4], r1_all[:, b0:b0 + 4],
                                        1.0, EPS, ALU.mult,
                                        ALU.add).then_inc(sDve.h, 1)
                        sDve.n += 1

                    @sec("scalar")
                    def _(s, b0=b0):
                        s.wait_ge(sDve.h, sDve.n)
                        ch(s, sAct, s.sqrt(r1_all[:, b0:b0 + 4],
                                           r1_all[:, b0:b0 + 4]))
                        s.sqrt(r2_all[:, b0:b0 + 4],
                               r2_all[:, b0:b0 + 4]).then_inc(sAct.h, 1)
                        sAct.n += 1
                    fin_act = sAct.n

                    @sec("vector")
                    def _(v, b0=b0, fin_act=fin_act):
                        v.wait_ge(sAct.h, fin_act)
                        ch(v, sDve, v.reciprocal(r1_all[:, b0:b0 + 4],
                                                 r1_all[:, b0:b0 + 4]))
                        v.reciprocal(r2_all[:, b0:b0 + 4],
                                     r2_all[:, b0:b0 + 4]).then_inc(sDve.h, 1)
                        sDve.n += 1
                    stats_done[tc // 4] = sDve.n

            for tcl0 in range(4):
                emit_p1chunk(tcl0, 0)
                emit_p1chunk_tr()
            emit_shards()
            dve_shards = sDve.n

            @sec("gpsimd")
            def _(gp):
                gp.wait_ge(sDve.h, dve_shards)
                gp.dma_start(out=absd.rearrange("a (p m) -> p (a m)", p=P),
                             in_=absacc[:, :]).then_inc(sGp.h, 16)
                sGp.n += 16
                gp.wait_ge(sGp.h, sGp.n)
                gp.dma_start(out=absrow[:, :], in_=absd).then_inc(sGp.h, 16)
                sGp.n += 16
            gp_absrow = sGp.n

            @sec("vector")
            def _(v):
                v.wait_ge(sGp.h, gp_absrow)
                ch(v, sDve, v.tensor_reduce(scrow[0:1, 0:1], absrow[0:1, 0:256:2],
                                            AXX, ALU.add))
                v.tensor_reduce(scrow[0:1, 1:2], absrow[0:1, 1:256:2], AXX,
                                ALU.add).then_inc(sDve.h, 1)
                sDve.n += 1
            dve_scrow = sDve.n

            @sec("gpsimd")
            def _(gp):
                gp.wait_ge(sDve.h, dve_scrow)
                gp.dma_start(out=ccs_i, in_=scrow[:, :]).then_inc(sGp.h, 16)
                sGp.n += 16
                gp.wait_ge(sGp.h, sGp.n)
                gp.collective_compute(
                    "AllReduce", ALU.add, replica_groups=[list(range(8))],
                    ins=[ccs_i.opt()], outs=[ccs_o.opt()],
                ).then_inc(sCc.h, 1)
                sCc.n += 1
                gp.wait_ge(sCc.h, sCc.n)
                gp.dma_start(out=thpos[:, :],
                             in_=ccs_o.partition_broadcast(P)).then_inc(sGp.h, 16)
                sGp.n += 16
            gp_scales = sGp.n

            @sec("vector")
            def _(v):
                v.wait_ge(sGp.h, gp_scales)
                ch(v, sDve, v.tensor_scalar(thpos[:, :], thpos[:, :], 1.0 / NW, 1e-8,
                                            ALU.mult, ALU.max))
                ch(v, sDve, v.tensor_scalar(thneg[:, :], thpos[:, :], -0.5, None,
                                            ALU.mult))
                v.tensor_scalar(thpos[:, :], thpos[:, :], 0.5, None,
                                ALU.mult).then_inc(sDve.h, 1)
                sDve.n += 1
            dve_thresh = sDve.n

            tile_order = list(range(16, 48)) + list(range(16))

            def jb_of_blk(b):
                return tile_order[2 * b] // 2

            wld_map = {}

            def emit_wstat_load(ltau, lblk):
                wsem = sWst[lblk % 4]

                @sec("scalar")
                def _(s, ltau=ltau, lblk=lblk, wsem=wsem):
                    jb = jb_of_blk(lblk)
                    if ltau == 0:
                        csem, ccnt = wprep_jb[jb]
                        s.wait_ge(csem.h, ccnt)
                    s.dma_start(out=wstat_v[:, lblk % 4],
                                in_=win16s_d[jb]).then_inc(wsem.h, 16)
                    wsem.n += 16
                wld_map[(ltau, lblk)] = (wsem, wsem.n)

            for bb in range(3):
                emit_wstat_load(0, bb)

            for tau in range(NTAU):
                slot = tau % 2
                tile_idx = 0

                for blk in range(24):
                    half = blk % 4

                    for sub in range(2):
                        j = tile_order[2 * blk + sub]
                        bank = tile_idx % 5
                        tile_idx += 1
                        wsem, wld = wld_map[(tau, blk)]

                        @sec("tensor")
                        def _(te, j=j, sub=sub, half=half, bank=bank, slot=slot,
                              wsem=wsem, wld=wld, blk=blk, tau=tau):
                            te.wait_ge(wsem.h, wld)
                            te.wait_ge(sAct.h, ev["xn2T_act"][slot])
                            if ev["qkv_act"][bank]:
                                te.wait_ge(sAct.h, ev["qkv_act"][bank])
                            group = [(psA[:, bank * 512:(bank + 1) * 512],
                                      wstat_v[:, half, d, sub * P:(sub + 1) * P],
                                      xn2T_v[:, slot, d, :]) for d in range(ND)]
                            mm_group(te, group, inc=sPe)
                        ev["wstat_pe"][0] = sPe.n

                        if sub == 1:
                            if blk + 3 < 24:
                                emit_wstat_load(tau, blk + 3)
                            elif tau + 1 < NTAU:
                                emit_wstat_load(tau + 1, blk + 3 - 24)

                        @sec("scalar")
                        def _(s, j=j, bank=bank):
                            s.wait_ge(sPe.h, sPe.n)
                            s.wait_ge(sCst.h, cst_all)
                            src = psA[:, bank * 512:(bank + 1) * 512]
                            if j < 16:
                                qslot = j % 2
                                if ev["qs_dma"][qslot]:
                                    s.wait_ge(sQs[qslot].h, ev["qs_dma"][qslot])
                                instr = s.activation(
                                    qstage[:, qslot * 512:(qslot + 1) * 512], src,
                                    AF.Identity, bias=binpp[:, j:j + 1])
                            elif j < 32:
                                instr = s.activation(
                                    KTf[:, (j - 16) * 512:(j - 15) * 512], src,
                                    AF.Identity, bias=binpp[:, j:j + 1])
                            else:
                                instr = s.activation(
                                    VTf[:, (j - 32) * 512:(j - 31) * 512], src,
                                    AF.Identity, bias=binpp[:, j:j + 1])
                            instr.then_inc(sAct.h, 1)
                            sAct.n += 1
                        ev["qkv_act"][bank] = sAct.n

                        if j < 16:
                            qslot = j % 2

                            @sec("sync")
                            def _(sy, j=j, tau=tau, qslot=qslot):
                                sy.wait_ge(sAct.h, sAct.n)
                                sy.dma_start(
                                    out=qts_d[tau, :, j, :],
                                    in_=qstage[:, qslot * 512:(qslot + 1) * 512],
                                ).then_inc(sQs[qslot].h, 16)
                                sQs[qslot].n += 16
                            ev["qs_dma"][qslot] = sQs[qslot].n

                    if blk == 15:
                        ev["kv_act_done"] = sAct.n
                    if blk == 1 and tau > 0:
                        emit_kv_back()
                    if blk in (4, 9, 14, 19) and tau + 1 < NTAU:
                        emit_p1chunk((tau + 1) * 4 + (4, 9, 14, 19).index(blk),
                                     (tau + 1) % 2)
                    if blk in (6, 11, 16, 21) and tau + 1 < NTAU:
                        emit_p1chunk_tr()
                    if blk in (19, 21, 23):
                        emit_kv_back()
                    if blk in (17, 19, 21, 23):
                        emit_kv_chunk(tau, (blk - 17) // 2)
                    emit_quant_piece()
                    emit_quant_piece()
                    if blk % 3 != 0:
                        emit_quant_piece()

            emit_kv_back()
            while qstate["i"] < NQ:
                emit_quant_piece()
            quant_w1_done = [qstate["st_after"][126], qstate["st_after"][127]]
            quant_all_done = [qstate["st_after"][254], qstate["st_after"][255]]

            a_dve_done = sDve.n

            @sec("gpsimd")
            def _(gp):
                gp.wait_ge(sDve.h, a_dve_done)
                gp.dma_start(out=cca_i, in_=A_acc[:, :]).then_inc(sGp.h, 16)
                sGp.n += 16
                gp.wait_ge(sGp.h, sGp.n)
                gp.collective_compute(
                    "AllReduce", ALU.add,
                    replica_groups=[[0, 1], [2, 3], [4, 5], [6, 7]],
                    ins=[cca_i.opt()], outs=[cca_o.opt()],
                ).then_inc(sCc.h, 1)
                sCc.n += 1
                gp.wait_ge(sCc.h, sCc.n)
                gp.dma_start(out=A_one[:, :], in_=cca_o).then_inc(sGp.h, 16)
                sGp.n += 16
            gp_A_done = sGp.n

            @sec("sync")
            def _(sy):
                for sem in [sX1, sLd, sCs[0], sCs[1], sQs[0], sQs[1], sQst[0],
                            sQst[1], sQst2[0], sQst2[1], sSh[0], sSh[1], sCst]:
                    if sem.n:
                        sy.wait_ge(sem.h, sem.n)
                sy.nop().then_inc(sPh.h, 1)
                sPh.n += 1

            @sec("scalar")
            def _(s):
                for sem in sWst + sQr + [sSc]:
                    if sem.n:
                        s.wait_ge(sem.h, sem.n)
                s.nop().then_inc(sPh.h, 1)
                sPh.n += 1

            @sec("gpsimd")
            def _(gp):
                for sem in [sGp, sGpC, sWprep]:
                    if sem.n:
                        gp.wait_ge(sem.h, sem.n)
                gp.nop().then_inc(sPh.h, 1)
                sPh.n += 1

            @sec("vector")
            def _(v):
                v.nop().then_inc(sPh.h, 1)
                sPh.n += 1

            @sec("tensor")
            def _(te):
                for sem in sWst:
                    if sem.n:
                        te.wait_ge(sem.h, sem.n)
                te.nop().then_inc(sPh.h, 1)
                sPh.n += 1

        with contextlib.ExitStack() as p2:
            E2 = p2.enter_context
            hT = E2(nc.sbuf_tensor("hT", [P, NF * 512], BF16))
            xnT = E2(nc.sbuf_tensor("xnT", [P, ND * 512], BF16))
            YT = E2(nc.sbuf_tensor("YT", [P, ND * 512], BF16))
            gT = E2(nc.sbuf_tensor("gT", [P, 16 * 512], BF16))
            gtok4 = E2(nc.sbuf_tensor("gtok4", [P, 4 * D], BF16))
            wbuf = E2(nc.sbuf_tensor("wbuf", [P, 2 * NF * P], BF16))
            xstage2 = E2(nc.sbuf_tensor("xstage2", [P, D], F32))
            xnb2 = E2(nc.sbuf_tensor("xnb2", [P, D], BF16))
            buf1 = E2(nc.sbuf_tensor("buf1", [P, D], F32))
            s_tile = E2(nc.sbuf_tensor("s_tile", [P, 4 * D], BF16))

            wbuf_v = wbuf[:, :].rearrange("p (b c) -> p b c", b=2)
            xnT_v = xnT[:, :].rearrange("p (n t) -> p n t", n=ND)
            YT_v = YT[:, :].rearrange("p (n t) -> p n t", n=ND)
            hT_v = hT[:, :].rearrange("p (n t) -> p n t", n=NF)
            out_v = out_d.rearrange("(n p) m -> n p m", p=P)

            for eng in ["tensor", "scalar", "sync", "gpsimd", "vector"]:
                @sec(eng)
                def _(e):
                    e.wait_ge(sPh.h, 5)

            NBANK = 3
            ev2 = {
                "xs_dve": 0,
                "prep_pe": 0,
                "psT_act": 0,
                "xnT_act": 0,
                "wbuf_pe": [0, 0],
                "bank_act": [0] * NBANK,
                "yt_pe": 0,
                "yt_dve": 0,
                "gtok_act": 0,
                "comb_dve": 0,
                "s_act": 0,
                "out_st": 0,
                "bank": 0,
            }

            def nextbank():
                b = ev2["bank"]
                ev2["bank"] = (b + 1) % NBANK
                return b

            def emit_prep_chunk(tau, tcl):
                tc = tau * 4 + tcl

                @sec("sync")
                def _(sy, tc=tc):
                    if ev2["xs_dve"]:
                        sy.wait_ge(sDve.h, ev2["xs_dve"])
                    if ev2["out_st"]:
                        sy.wait_ge(sSt.h, ev2["out_st"])
                    sy.dma_start(out=xstage2[:, :], in_=xd_v[tc]).then_inc(sX2.h, 16)
                    sX2.n += 16
                x_ld = sX2.n

                @sec("vector")
                def _(v, tc=tc, x_ld=x_ld):
                    v.wait_ge(sX2.h, x_ld)
                    if ev2["prep_pe"]:
                        v.wait_ge(sPe.h, ev2["prep_pe"])
                    v.tensor_scalar(xnb2[:, :], xstage2[:, :], m_all[:, tc:tc + 1],
                                    r1_all[:, tc:tc + 1], ALU.subtract,
                                    ALU.mult).then_inc(sDve.h, 1)
                    sDve.n += 1
                ev2["xs_dve"] = sDve.n
                ln_dve = sDve.n

                @sec("tensor")
                def _(te, ln_dve=ln_dve):
                    te.wait_ge(sDve.h, ln_dve)
                    te.wait_ge(sGpC.h, gpc_all)
                    if ev2["psT_act"]:
                        te.wait_ge(sAct.h, ev2["psT_act"])
                    for dt in range(ND - 1):
                        te.transpose(psT2k[:, dt * P:(dt + 1) * P],
                                     xnb2[:, dt * P:(dt + 1) * P], ident16[:, :])
                    te.transpose(psT2k[:, (ND - 1) * P:ND * P],
                                 xnb2[:, (ND - 1) * P:ND * P],
                                 ident16[:, :]).then_inc(sPe.h, 1)
                    sPe.n += 1
                ev2["prep_pe"] = sPe.n

                @sec("scalar")
                def _(s, tcl=tcl):
                    s.wait_ge(sPe.h, sPe.n)
                    s.copy(xnT_v[:, :, tcl * P:(tcl + 1) * P], psT2k_v).then_inc(sAct.h, 1)
                    sAct.n += 1
                ev2["psT_act"] = sAct.n
                if tcl == 3:
                    ev2["xnT_act"] = sAct.n

            def emit_ytp(tau):
                @sec("sync")
                def _(sy, tau=tau):
                    if ev2["yt_pe"]:
                        sy.wait_ge(sPe.h, ev2["yt_pe"])
                    sy.dma_start(out=YT_v, in_=qts_d[tau]).then_inc(sYt.h, 16)
                    sYt.n += 16
                qt_ld = sYt.n

                @sec("vector")
                def _(v, qt_ld=qt_ld):
                    v.wait_ge(sYt.h, qt_ld)
                    v.wait_ge(sDve.h, aone_dve)
                    for dt in range(ND - 1):
                        v.tensor_scalar(YT_v[:, dt, :], YT_v[:, dt, :],
                                        A_one[:, dt:dt + 1], None, ALU.mult)
                    v.tensor_scalar(YT_v[:, ND - 1, :], YT_v[:, ND - 1, :],
                                    A_one[:, ND - 1:ND], None,
                                    ALU.mult).then_inc(sDve.h, 1)
                    sDve.n += 1
                ev2["yt_dve"] = sDve.n

            def emit_f1(tau):
                for fb in range(16):
                    whalf = fb % 2

                    @sec("gpsimd")
                    def _(gp, fb=fb, whalf=whalf, tau=tau):
                        if ev2["wbuf_pe"][whalf]:
                            gp.wait_ge(sPe.h, ev2["wbuf_pe"][whalf])
                        if tau == 0 and fb < 2:
                            for _sem, _val in quant_w1_done:
                                gp.wait_ge(_sem.h, _val)
                        gp.dma_start(
                            out=wbuf_v[:, whalf, 0:ND * 512].rearrange("p (n m) -> p n m", n=ND),
                            in_=w1q_d[fb]).then_inc(sWb[whalf].h, 16)
                        sWb[whalf].n += 16
                    w_ld = sWb[whalf].n

                    for sub in range(4):
                        ft = fb * 4 + sub
                        bank = nextbank()

                        @sec("tensor")
                        def _(te, sub=sub, whalf=whalf, bank=bank, w_ld=w_ld):
                            te.wait_ge(sWb[whalf].h, w_ld)
                            te.wait_ge(sAct.h, ev2["xnT_act"])
                            if ev2["bank_act"][bank]:
                                te.wait_ge(sAct.h, ev2["bank_act"][bank])
                            group = [(psA[:, bank * 512:(bank + 1) * 512],
                                      wbuf_v[:, whalf, d * 512 + sub * P:d * 512 + (sub + 1) * P],
                                      xnT_v[:, d, :]) for d in range(ND)]
                            mm_group(te, group, inc=sPe)
                        ev2["wbuf_pe"][whalf] = sPe.n

                        @sec("scalar")
                        def _(s, ft=ft, bank=bank):
                            s.wait_ge(sPe.h, sPe.n)
                            s.wait_ge(sCst.h, cst_all)
                            s.activation(hT_v[:, ft, :],
                                         psA[:, bank * 512:(bank + 1) * 512],
                                         AF.Gelu, bias=b1pp[:, ft:ft + 1],
                                         scale=-1.0).then_inc(sAct.h, 1)
                            sAct.n += 1
                        ev2["bank_act"][bank] = sAct.n

            def emit_f2(tau, prep_tau):
                f1_done = sAct.n
                for et in range(16):
                    whalf = et % 2
                    bank = nextbank()

                    @sec("gpsimd")
                    def _(gp, et=et, whalf=whalf, tau=tau):
                        gp.wait_ge(sPe.h, ev2["wbuf_pe"][whalf])
                        if tau == 0 and et < 2:
                            for _sem, _val in quant_all_done:
                                gp.wait_ge(_sem.h, _val)
                        gp.dma_start(
                            out=wbuf_v[:, whalf, :].rearrange("p (n m) -> p n m", n=NF),
                            in_=w2q_d[et]).then_inc(sWb[whalf].h, 16)
                        sWb[whalf].n += 16
                    w_ld = sWb[whalf].n

                    @sec("tensor")
                    def _(te, et=et, whalf=whalf, bank=bank, w_ld=w_ld, f1_done=f1_done):
                        te.wait_ge(sWb[whalf].h, w_ld)
                        if et == 0:
                            te.wait_ge(sAct.h, f1_done)
                        if ev2["bank_act"][bank]:
                            te.wait_ge(sAct.h, ev2["bank_act"][bank])
                        group = [(psA[:, bank * 512:(bank + 1) * 512],
                                  wbuf_v[:, whalf, f * P:(f + 1) * P],
                                  hT_v[:, f, :]) for f in range(NF)]
                        mm_group(te, group, inc=sPe)
                    ev2["wbuf_pe"][whalf] = sPe.n

                    @sec("scalar")
                    def _(s, et=et, bank=bank):
                        s.wait_ge(sPe.h, sPe.n)
                        s.wait_ge(sCst.h, cst_all)
                        s.activation(gT[:, et * 512:(et + 1) * 512],
                                     psA[:, bank * 512:(bank + 1) * 512],
                                     AF.Silu, bias=b2pp[:, et:et + 1],
                                     scale=-1.0).then_inc(sAct.h, 1)
                        sAct.n += 1
                    ev2["bank_act"][bank] = sAct.n

                    if prep_tau is not None and et in (2, 5, 8, 11):
                        emit_prep_chunk(prep_tau, (2, 5, 8, 11).index(et))

            def emit_gtr(tau):
                f2_done = sAct.n
                for tcl in range(4):
                    @sec("tensor")
                    def _(te, tcl=tcl, f2_done=f2_done):
                        if tcl == 0:
                            te.wait_ge(sAct.h, f2_done)
                        if ev2["psT_act"]:
                            te.wait_ge(sAct.h, ev2["psT_act"])
                        for et in range(15):
                            te.transpose(psT2k[:, et * P:(et + 1) * P],
                                         gT[:, et * 512 + tcl * P:et * 512 + (tcl + 1) * P],
                                         ident16[:, :])
                        te.transpose(psT2k[:, 15 * P:16 * P],
                                     gT[:, 15 * 512 + tcl * P:15 * 512 + (tcl + 1) * P],
                                     ident16[:, :]).then_inc(sPe.h, 1)
                        sPe.n += 1

                    @sec("scalar")
                    def _(s, tcl=tcl):
                        s.wait_ge(sPe.h, sPe.n)
                        if ev2["comb_dve"]:
                            s.wait_ge(sDve.h, ev2["comb_dve"])
                        s.copy(gtok4[:, tcl * D:(tcl + 1) * D], psT2k).then_inc(sAct.h, 1)
                        sAct.n += 1
                    ev2["psT_act"] = sAct.n
                ev2["gtok_act"] = sAct.n

            def emit_w(tau):
                for et in range(4):
                    whalf = et % 2

                    @sec("gpsimd")
                    def _(gp, et=et, whalf=whalf):
                        gp.wait_ge(sPe.h, ev2["wbuf_pe"][whalf])
                        gp.dma_start(
                            out=wbuf_v[:, whalf, 0:ND * 512].rearrange("p (n m) -> p n m", n=ND),
                            in_=wout16s_d[et]).then_inc(sWb[whalf].h, 16)
                        sWb[whalf].n += 16
                    w_ld = sWb[whalf].n

                    for tcl in range(4):
                        bank = nextbank()

                        @sec("tensor")
                        def _(te, et=et, tcl=tcl, whalf=whalf, bank=bank, w_ld=w_ld):
                            te.wait_ge(sWb[whalf].h, w_ld)
                            te.wait_ge(sGpC.h, gpc_all)
                            te.wait_ge(sPool.h, pool_consts)
                            te.wait_ge(sDve.h, ev2["yt_dve"])
                            if ev2["bank_act"][bank]:
                                te.wait_ge(sAct.h, ev2["bank_act"][bank])
                            group = [(psA[:, bank * 512:(bank + 1) * 512],
                                      YT_v[:, d, tcl * P:(tcl + 1) * P],
                                      wbuf_v[:, whalf, d * 512:(d + 1) * 512])
                                     for d in range(ND)]
                            group.append((psA[:, bank * 512:(bank + 1) * 512],
                                          ones_r16[:, :],
                                          bout16[:, et * 512:(et + 1) * 512]))
                            mm_group(te, group, inc=sPe)
                        ev2["wbuf_pe"][whalf] = sPe.n

                        @sec("scalar")
                        def _(s, et=et, tcl=tcl, bank=bank):
                            s.wait_ge(sPe.h, sPe.n)
                            if ev2["comb_dve"]:
                                s.wait_ge(sDve.h, ev2["comb_dve"])
                            s.copy(s_tile[:, tcl * D + et * 512:tcl * D + (et + 1) * 512],
                                   psA[:, bank * 512:(bank + 1) * 512]).then_inc(sAct.h, 1)
                            sAct.n += 1
                        ev2["bank_act"][bank] = sAct.n
                ev2["yt_pe"] = sPe.n
                ev2["s_act"] = sAct.n

            def emit_comb(tau):
                for tcl in range(4):
                    tc = tau * 4 + tcl

                    @sec("sync")
                    def _(sy, tc=tc):
                        if ev2["xs_dve"]:
                            sy.wait_ge(sDve.h, ev2["xs_dve"])
                        if ev2["out_st"]:
                            sy.wait_ge(sSt.h, ev2["out_st"])
                        sy.dma_start(out=xstage2[:, :], in_=xd_v[tc]).then_inc(sX2.h, 16)
                        sX2.n += 16
                    x_ld = sX2.n

                    @sec("vector")
                    def _(v, tc=tc, tcl=tcl, x_ld=x_ld):
                        v.wait_ge(sX2.h, x_ld)
                        v.wait_ge(sAct.h, max(ev2["gtok_act"], ev2["s_act"]))
                        ch(v, sDve, v.tensor_scalar(buf1[:, :], xstage2[:, :],
                                        m_all[:, tc:tc + 1],
                                        r1_all[:, tc:tc + 1], ALU.subtract, ALU.mult))
                        ch(v, sDve, v.tensor_tensor(buf1[:, :], buf1[:, :],
                                        s_tile[:, tcl * D:(tcl + 1) * D], ALU.add))
                        ch(v, sDve, v.tensor_tensor(buf1[:, :], buf1[:, :],
                                        gtok4[:, tcl * D:(tcl + 1) * D], ALU.mult))
                        v.tensor_tensor(xstage2[:, :], buf1[:, :], xstage2[:, :],
                                        ALU.add).then_inc(sDve.h, 1)
                        sDve.n += 1
                    ev2["xs_dve"] = sDve.n

                    @sec("sync")
                    def _(sy, tc=tc):
                        sy.wait_ge(sDve.h, sDve.n)
                        sy.dma_start(out=out_v[tc], in_=xstage2[:, :]).then_inc(sSt.h, 16)
                        sSt.n += 16
                    ev2["out_st"] = sSt.n
                ev2["comb_dve"] = sDve.n

            emit_prep_chunk(0, 0)
            emit_prep_chunk(0, 1)
            emit_prep_chunk(0, 2)
            emit_prep_chunk(0, 3)

            @sec("vector")
            def _(v):
                v.wait_ge(sGp.h, gp_A_done)
                v.tensor_scalar(A_one[:, :], A_one[:, :], 0.5, 0.5,
                                ALU.mult, ALU.add).then_inc(sDve.h, 1)
                sDve.n += 1
            aone_dve = sDve.n

            emit_ytp(0)
            for tau in range(NTAU):
                emit_f1(tau)
                emit_f2(tau, tau + 1 if tau + 1 < NTAU else None)
                emit_gtr(tau)
                emit_w(tau)
                if tau + 1 < NTAU:
                    emit_ytp(tau + 1)
                emit_comb(tau)

            @sec("sync")
            def _(sy):
                sy.wait_ge(sSt.h, sSt.n)

            @sec("gpsimd")
            def _(gp):
                gp.wait_ge(sSt.h, sSt.n)

    return nc


_NC_CACHE = {}


def kernel(L0_attn_X, L0_attn_W_in, L0_attn_b_in, L0_attn_W_out, L0_attn_b_out,
           L0_attn_gamma, L0_attn_beta, L0_ffn_w1, L0_ffn_b1, L0_ffn_w2, L0_ffn_b2):
    X = np.ascontiguousarray(np.asarray(L0_attn_X, np.float32))
    W_in = np.ascontiguousarray(np.asarray(L0_attn_W_in, np.float32))
    b_in = np.asarray(L0_attn_b_in, np.float32).reshape(-1)
    W_out = np.ascontiguousarray(np.asarray(L0_attn_W_out, np.float32))
    b_out = np.asarray(L0_attn_b_out, np.float32).reshape(-1)
    gamma = np.asarray(L0_attn_gamma, np.float32).reshape(-1)
    beta = np.asarray(L0_attn_beta, np.float32).reshape(-1)
    w1 = np.asarray(L0_ffn_w1, np.float32)
    b1 = np.asarray(L0_ffn_b1, np.float32).reshape(-1)
    w2 = np.asarray(L0_ffn_w2, np.float32)
    b2 = np.asarray(L0_ffn_b2, np.float32).reshape(-1)

    w1t = np.ascontiguousarray(w1.T)
    w2t = np.ascontiguousarray(w2.T)
    win_s = np.ascontiguousarray(
        W_in.reshape(16, P, 24, 256).transpose(1, 2, 0, 3))
    wout_s = np.ascontiguousarray(
        W_out.reshape(16, P, 8, 256).transpose(1, 2, 0, 3))
    binpp = np.ascontiguousarray(b_in.reshape(48, P).T)
    b1pp = np.ascontiguousarray(b1.reshape(64, P).T)
    b2pp = np.ascontiguousarray(b2.reshape(16, P).T)
    iden = np.eye(P, dtype=np.float32)
    inv_freq = 1.0 / (10000.0 ** (np.arange(0, 1024, 2, dtype=np.float64) / 1024.0))

    if "nc" not in _NC_CACHE:
        _NC_CACHE["nc"] = build_nc()
    nc = _NC_CACHE["nc"]

    in_maps = []
    for c in range(8):
        b, h = c // 2, c % 2
        pos = np.arange(h * T, (h + 1) * T, dtype=np.float64)
        freqs = pos[:, None] * inv_freq[None, :]
        in_maps.append(dict(
            x=np.ascontiguousarray(X[b, h * T:(h + 1) * T, :]),
            w_in=win_s, w_out=wout_s, w1t=w1t, w2t=w2t,
            w1sh=np.ascontiguousarray(w1t[:, c * 1024:(c + 1) * 1024]),
            w2sh=np.ascontiguousarray(w2t[c * 1024:(c + 1) * 1024, :]),
            gam=np.ascontiguousarray(np.broadcast_to(gamma.reshape(1, D), (P, D))),
            bet=np.ascontiguousarray(np.broadcast_to(beta.reshape(1, D), (P, D))),
            b1pp=b1pp, b2pp=b2pp, bout=b_out.reshape(1, D),
            binpp=binpp,
            cosT=np.cos(freqs).astype(np.float32),
            sinT=np.sin(freqs).astype(np.float32),
            iden=iden,
        ))

    res = run_bass_kernel_spmd(nc, in_maps, core_ids=list(range(8)))
    out = np.empty((4, 4096, D), np.float32)
    for c in range(8):
        b, h = c // 2, c % 2
        out[b, h * T:(h + 1) * T, :] = res.results[c]["out"]
    return out
